# version 14 (final)
# baseline (speedup 1.0000x reference)
import numpy as np
import ml_dtypes
import concourse.bass as bass
import concourse.mybir as mybir
from concourse.bass_utils import run_bass_kernel_spmd

F32 = mybir.dt.float32
F32R = mybir.dt.float32r
BF16 = mybir.dt.bfloat16
FP8 = mybir.dt.float8e4
AF = mybir.ActivationFunctionType
ALU = mybir.AluOpType
AX = mybir.AxisListType
BF = ml_dtypes.bfloat16

B, S, C = 4, 8192, 1024
SOWN = 4096
MARG = 128
SEXT = SOWN + 2 * MARG
NT = SEXT // 128
NST_ALL = S // 512
NST_EXT = 9
SUPS = [(u * 4, 4) for u in range(8)] + [(32, 2)]
H0, D0 = 32, 32
H1, D1 = 16, 64
EPS0 = 1e-15
LN_EPS = 1e-5
CLAMP = 80.0
NEG = -1e30
INV_C = 1.0 / C

BUILD_UPTO = 6
DEBUG_TAPS = False


class _Fence:

    def __init__(self, eng, sem):
        self.eng = eng
        self.sem = sem
        self.n = 0

    def __call__(self, ins):
        self.n += 1
        ins.then_inc(self.sem, 1)
        self.eng.wait_ge(self.sem, self.n)
        return ins


def _quant_dve_pre(v, src, qs, qs2, qrec, qu, fence):
    fence(v.reduce_max(qs[:, :], src.rearrange("p (b i) -> p b i", i=32),
                       axis=AX.X, apply_absolute_value=True))
    fence(v.tensor_scalar(qs2[:, :], qs[:, :], 1.0 / 7.5, 1e-30, ALU.mult,
                          ALU.max))
    fence(v.reciprocal(qrec[:, :], qs2[:, :]))
    return v.tensor_tensor(
        qu.rearrange("p (b i) -> p b i", i=32),
        src.rearrange("p (b i) -> p b i", i=32),
        qrec[:, :].rearrange("p (b o) -> p b o", o=1).broadcast_to((128, 32, 32)),
        ALU.mult)


def _quant_dve_post(v, qau, qsg, qu, qv8, qs2, dst):
    v.tensor_single_scalar(qau[:, :], qau[:, :], 1.0, ALU.is_lt)
    v.tensor_mul(qsg[:, :], qsg[:, :], qau[:, :])
    v.tensor_add(qu[:, :], qu[:, :], qsg[:, :])
    v.tensor_copy(qv8[:, :], qu[:, :])
    v.tensor_tensor(qu[:, :], qv8[:, :], qsg[:, :], ALU.subtract)
    return v.tensor_tensor(
        dst.rearrange("p (b i) -> p b i", i=32),
        qu.rearrange("p (b i) -> p b i", i=32),
        qs2[:, :].rearrange("p (b o) -> p b o", o=1).broadcast_to((128, 32, 32)),
        ALU.mult)


def build_program():
    nc = bass.Bass("TRN2", target_bir_lowering=False,
                   detect_race_conditions=False)
    n = nc
    dbg = "ExternalOutput" if DEBUG_TAPS else "Internal"

    x_all = n.dram_tensor("x_all", [S, C], F32, kind="ExternalInput").ap()
    x_ext = n.dram_tensor("x_ext", [SEXT, C], F32, kind="ExternalInput").ap()
    w_qkv = n.dram_tensor("w_qkv", [3 * C, C], F32, kind="ExternalInput").ap()
    w_proj = n.dram_tensor("w_proj", [C, C], F32, kind="ExternalInput").ap()
    b_proj = n.dram_tensor("b_proj", [1, C], F32, kind="ExternalInput").ap()
    w_ffn = n.dram_tensor("w_ffn", [C, C], F32, kind="ExternalInput").ap()
    b_ffn = n.dram_tensor("b_ffn", [1, C], F32, kind="ExternalInput").ap()
    ones128 = n.dram_tensor("ones128", [1, 512], F32, kind="ExternalInput").ap()

    out_c = n.dram_tensor("out_c", [SOWN, C], F32, kind="ExternalOutput").ap()

    xbf_all = n.dram_tensor("xbf_all", [S, C], BF16, kind="Internal").ap()
    xbf_ext = n.dram_tensor("xbf_ext", [SEXT, C], BF16, kind="Internal").ap()
    wqkv_bf = n.dram_tensor("wqkv_bf", [3 * C, C], BF16, kind="Internal").ap()
    wproj_bf = n.dram_tensor("wproj_bf", [C, C], BF16, kind="Internal").ap()
    wd_bf = n.dram_tensor("wd_bf", [C, C], BF16, kind="Internal").ap()
    pin_bf = n.dram_tensor("pin_bf", [SEXT, C], BF16, kind=dbg).ap()
    xd_bf = n.dram_tensor("xd_bf", [SEXT, C], BF16, kind=dbg).ap()
    x2_tm = n.dram_tensor("x2_tm", [SEXT, C], F32, kind=dbg).ap()
    x2_bf = n.dram_tensor("x2_bf", [SEXT, C], BF16, kind="Internal").ap()

    vk_tap = n.dram_tensor("vk_tap", [128, 8 * 132], BF16, kind=dbg).ap()

    if BUILD_UPTO >= 3:
        wq_i = n.dram_tensor("wq_i", [C, C], F32, kind="ExternalInput").ap()
        wk_i = n.dram_tensor("wk_i", [C, C], F32, kind="ExternalInput").ap()
        wv_i = n.dram_tensor("wv_i", [C, C], F32, kind="ExternalInput").ap()
        bv_i = n.dram_tensor("bv_i", [1, C], F32, kind="ExternalInput").ap()
        bqc_i = n.dram_tensor("bqc_i", [128, 8], F32, kind="ExternalInput").ap()
        bkc_i = n.dram_tensor("bkc_i", [128, 8], F32, kind="ExternalInput").ap()
        qfm_d = n.dram_tensor("qfm_d", [C, SEXT], BF16, kind=dbg).ap()
        kfm_d = n.dram_tensor("kfm_d", [C, SEXT], BF16, kind=dbg).ap()
        vpad_d = n.dram_tensor("vpad_d", [SEXT, H1 * (D1 + 1)], BF16,
                               kind=dbg).ap()
    if BUILD_UPTO >= 4:
        biasT_i = n.dram_tensor("biasT_i", [H1, 3, 128, 128], F32,
                                kind="ExternalInput").ap()
        biasTL_i = n.dram_tensor("biasTL_i", [H1, 128, 128], F32,
                                 kind="ExternalInput").ap()
        biasTR_i = n.dram_tensor("biasTR_i", [H1, 128, 128], F32,
                                 kind="ExternalInput").ap()
        identb = n.dram_tensor("identb", [128, 128], BF16,
                               kind="ExternalInput").ap()
        attn_bf = n.dram_tensor("attn_bf", [SOWN, C], BF16, kind=dbg).ap()
    if BUILD_UPTO >= 5:
        wo_i = n.dram_tensor("wo_i", [C, C], F32, kind="ExternalInput").ap()
        bo_i = n.dram_tensor("bo_i", [1, C], F32, kind="ExternalInput").ap()
        x3_tm = n.dram_tensor("x3_tm", [SOWN, C], F32, kind=dbg).ap()
        x3_bf = n.dram_tensor("x3_bf", [SOWN, C], BF16, kind="Internal").ap()
    if BUILD_UPTO >= 6:
        wr_i = n.dram_tensor("wr_i", [C, 8], F32, kind="ExternalInput").ap()
        br_i = n.dram_tensor("br_i", [1, 8], F32, kind="ExternalInput").ap()
        w1_i = n.dram_tensor("w1_i", [8, C, 2048], F32,
                             kind="ExternalInput").ap()
        b1_col = n.dram_tensor("b1_col", [128, 8 * 16], F32,
                               kind="ExternalInput").ap()
        w2_i = n.dram_tensor("w2_i", [8, 2048, C], F32,
                             kind="ExternalInput").ap()
        b2_i = n.dram_tensor("b2_i", [8, C], F32, kind="ExternalInput").ap()
        esel_i = n.dram_tensor("esel_i", [8, 8 * 128], BF16,
                               kind="ExternalInput").ap()

    vkbd = n.alloc_sbuf_tensor("vkbd", [128, 8 * 132], BF16)
    onesr = n.alloc_sbuf_tensor("onesr", [1, 512], F32R)

    _semsnap = [None]

    def snapsems():
        _semsnap[0] = n.snapshot_sems()

    def freesems():
        n.clear_and_free_semaphores(n.allocated_since(_semsnap[0]))

    wctx = [
        n.sbuf_tensor("wp_sb", [128, 8 * 1024], BF16),
        n.sbuf_tensor("wd_sb", [128, 8 * 1024], BF16),
        n.sbuf_tensor("bp_row", [1, 1024], F32R),
        n.sbuf_tensor("bfn_row", [1, 1024], F32R),
    ]
    wp_sb, wd_sb, bp_row, bfn_row = [c.__enter__() for c in wctx]

    qfm_g = n.sbuf_tensor("qfm", [128, 8 * SEXT], BF16)
    qfm = qfm_g.__enter__()

    snapsems()
    with (
        n.sbuf_tensor("wkv_sb", [128, 8 * 2048], BF16) as wkv_sb,
        n.sbuf_tensor("wq_sb", [128, 8 * 1024], BF16) as wq_sb,
        n.sbuf_tensor("xfm0", [128, 8 * 512], BF16) as xfm0,
        n.sbuf_tensor("xfm1", [128, 8 * 512], BF16) as xfm1,
        n.sbuf_tensor("ktm0", [128, 1024], BF16) as ktm0,
        n.sbuf_tensor("ktm1", [128, 1024], BF16) as ktm1,
        n.sbuf_tensor("vpad0", [128, 1056], BF16) as vpad0,
        n.sbuf_tensor("vpad1", [128, 1056], BF16) as vpad1,
        n.psum_tensor("psKV0", [128, 512], F32) as psKV0,
        n.psum_tensor("psKV1", [128, 512], F32) as psKV1,
        n.psum_tensor("psVKa", [128, 512], F32) as psVKa,
        n.psum_tensor("psVKb", [128, 512], F32) as psVKb,
        n.psum_tensor("psVKc", [128, 512], F32) as psVKc,
        n.psum_tensor("psQ0", [128, 512], F32) as psQ0,
        n.psum_tensor("psQ1", [128, 512], F32) as psQ1,
        n.Block("p1a") as blk,
    ):
        xfm = [xfm0, xfm1]
        ktm = [ktm0, ktm1]
        vpad = [vpad0, vpad1]
        psKV = [psKV0, psKV1]
        psQ = [psQ0, psQ1]
        vk_ps = [(psVKa, 0), (psVKa, 132), (psVKa, 264),
                 (psVKb, 0), (psVKb, 132), (psVKb, 264),
                 (psVKc, 0), (psVKc, 132)]

        s_cast = n.alloc_semaphore("s_cast")
        s_xc = n.alloc_semaphore("s_xc")
        s_wt = n.alloc_semaphore("s_wt")
        s_xt = n.alloc_semaphore("s_xt")
        s_kv = n.alloc_semaphore("s_kv")
        s_ev = n.alloc_semaphore("s_ev")
        s_vk = n.alloc_semaphore("s_vk")
        s_qm = n.alloc_semaphore("s_qm")
        s_qe = n.alloc_semaphore("s_qe")
        s_st = n.alloc_semaphore("s_st")

        def qst_T(qst):
            return 512 if qst < 8 else SEXT - 8 * 512

        @blk.gpsimd
        def _(g):
            g.dma_start(out=wqkv_bf[1024:3072, :],
                        in_=w_qkv[1024:3072, :]).then_inc(s_cast, 16)
            for ch in range(4):
                if ch >= 1:
                    g.wait_ge(s_xt, 128 * (4 * ch - 3))
                g.dma_start(out=xbf_all[ch * 2048:(ch + 1) * 2048, :],
                            in_=x_all[ch * 2048:(ch + 1) * 2048, :]
                            ).then_inc(s_xc, 16)
            g.wait_ge(s_xt, 128 * 14)
            g.dma_start(out=xbf_ext, in_=x_ext).then_inc(s_cast, 16)
            g.dma_start(out=wqkv_bf[0:1024, :],
                        in_=w_qkv[0:1024, :]).then_inc(s_cast, 16)
            g.dma_start(out=onesr[:, :], in_=ones128).then_inc(s_cast, 16)
            g.dma_start(out=wproj_bf, in_=w_proj).then_inc(s_cast, 16)

        @blk.sync
        def _(sy):
            sy.wait_ge(s_cast, 16)
            for kc in range(8):
                sy.dma_start_transpose(
                    wkv_sb[:, kc * 2048:(kc + 1) * 2048],
                    wqkv_bf[1024:3072, kc * 128:(kc + 1) * 128],
                ).then_inc(s_wt, 16)
            for st in range(NST_ALL):
                if st >= 2:
                    sy.wait_ge(s_vk, (st - 2) * 4 + 4)
                sy.wait_ge(s_xc, 16 * (st // 4 + 1))
                for kc in range(8):
                    sy.dma_start_transpose(
                        xfm[st % 2][:, kc * 512:(kc + 1) * 512],
                        xbf_all[st * 512:(st + 1) * 512,
                                kc * 128:(kc + 1) * 128],
                    ).then_inc(s_xt, 16)
            sy.wait_ge(s_cast, 48)
            for kc in range(8):
                sy.dma_start_transpose(
                    wq_sb[:, kc * 1024:(kc + 1) * 1024],
                    wqkv_bf[0:1024, kc * 128:(kc + 1) * 128],
                ).then_inc(s_wt, 16)
            for qst in range(NST_EXT):
                T = qst_T(qst)
                if qst < 2:
                    sy.wait_ge(s_vk, NST_ALL * 4)
                else:
                    sy.wait_ge(s_qm, (qst - 2) * 8 + 8)
                for kc in range(8):
                    sy.dma_start_transpose(
                        xfm[qst % 2][:, kc * 512:kc * 512 + T],
                        xbf_ext[qst * 512:qst * 512 + T,
                                kc * 128:(kc + 1) * 128],
                    ).then_inc(s_xt, 16)
            if DEBUG_TAPS:
                sy.wait_ge(s_vk, NST_ALL * 4 + 1)
                sy.dma_start(out=vk_tap, in_=vkbd[:, :]).then_inc(s_st, 16)
                sy.wait_ge(s_st, 16)
            sy.wait_ge(s_xt, 16 * 8 * (NST_ALL + NST_EXT))
            sy.wait_ge(s_wt, 16 * 16)
            sy.wait_ge(s_cast, 16 * 5)
            sy.wait_ge(s_xc, 16 * 4)

        @blk.tensor
        def _(t):
            t.wait_ge(s_wt, 16 * 8)
            for st in range(NST_ALL):
                t.wait_ge(s_xt, (st + 1) * 128)
                for tsub in range(4):
                    kvg0 = (st * 4 + tsub) * 4
                    for og in range(4):
                        kvg = kvg0 + og
                        if kvg >= 2:
                            t.wait_ge(s_ev, kvg - 1)
                        ins = None
                        for kc in range(8):
                            ins = t.matmul(
                                psKV[og % 2][:, :],
                                xfm[st % 2][:, kc * 512 + tsub * 128:
                                            kc * 512 + tsub * 128 + 128],
                                wkv_sb[:, kc * 2048 + og * 512:
                                       kc * 2048 + (og + 1) * 512],
                                start=(kc == 0), stop=(kc == 7),
                            )
                        ins.then_inc(s_kv, 1)
                    t.wait_ge(s_ev, kvg0 + 4)
                    par = tsub % 2
                    first = (st == 0 and tsub == 0)
                    last = (st == NST_ALL - 1 and tsub == 3)
                    ins = None
                    for gi in range(8):
                        pst, off = vk_ps[gi]
                        ins = t.matmul(
                            pst[:, off:off + 132],
                            ktm[par][:, gi * 128:(gi + 1) * 128],
                            vpad[par][:, gi * 132:(gi + 1) * 132],
                            start=(first and gi in (0, 3, 6)),
                            stop=(last and gi in (2, 5, 7)),
                        )
                    ins.then_inc(s_vk, 1)
            t.wait_ge(s_wt, 16 * 16)
            for qst in range(NST_EXT):
                T = qst_T(qst)
                t.wait_ge(s_xt, NST_ALL * 128 + (qst + 1) * 128)
                for oc in range(8):
                    qg = qst * 8 + oc
                    if qg >= 2:
                        t.wait_ge(s_qe, qg - 1)
                    ins = None
                    for kc in range(8):
                        ins = t.matmul(
                            psQ[oc % 2][:, 0:T],
                            wq_sb[:, kc * 1024 + oc * 128:
                                  kc * 1024 + (oc + 1) * 128],
                            xfm[qst % 2][:, kc * 512:kc * 512 + T],
                            start=(kc == 0), stop=(kc == 7),
                        )
                    ins.then_inc(s_qm, 1)

        @blk.scalar
        def _(s):
            for st in range(NST_ALL):
                for tsub in range(4):
                    par = tsub % 2
                    vkt = st * 4 + tsub
                    for og in range(4):
                        kvg = vkt * 4 + og
                        s.wait_ge(s_kv, kvg + 1)
                        if vkt >= 2 and og == 0:
                            s.wait_ge(s_vk, vkt - 1)
                        if og < 2:
                            s.activation(
                                ktm[par][:, og * 512:(og + 1) * 512],
                                psKV[og % 2][:, :], AF.Relu,
                            ).then_inc(s_ev, 1)
                        else:
                            hh0 = 16 * (og - 2)
                            s.activation(
                                vpad[par][:, :]
                                .rearrange("p (h e) -> p h e", e=33)
                                [:, hh0:hh0 + 16, 0:32],
                                psKV[og % 2][:, :]
                                .rearrange("p (h d) -> p h d", d=32),
                                AF.Relu,
                            ).then_inc(s_ev, 1)
            for qst in range(NST_EXT):
                T = qst_T(qst)
                for oc in range(8):
                    qg = qst * 8 + oc
                    s.wait_ge(s_qm, qg + 1)
                    s.activation(
                        qfm[:, oc * SEXT + qst * 512:
                            oc * SEXT + qst * 512 + T],
                        psQ[oc % 2][:, 0:T], AF.Relu,
                    ).then_inc(s_qe, 1)

        @blk.vector
        def _(v):
            for par in range(2):
                v.memset(
                    vpad[par][:, :].rearrange("p (h e) -> p h e", e=33)
                    [:, :, 32:33], 1.0)
            v.memset(vkbd[:, :], 0.0)
            v.wait_ge(s_vk, NST_ALL * 4)
            ins = None
            for gi in range(8):
                pst, off = vk_ps[gi]
                for j in range(4):
                    ins = v.tensor_copy(
                        vkbd[32 * j:32 * (j + 1),
                             gi * 132 + 33 * j:gi * 132 + 33 * (j + 1)],
                        pst[32 * j:32 * (j + 1),
                            off + 33 * j:off + 33 * (j + 1)],
                    )
            ins.then_inc(s_vk, 1)

    freesems()
    if BUILD_UPTO == 1:
        qfm_g.__exit__(None, None, None)
        for cx in reversed(wctx):
            cx.__exit__(None, None, None)
        return nc

    snapsems()
    with (
        n.sbuf_tensor("pden", [128, 64], F32) as pden,
        n.sbuf_tensor("ptm0", [128, 1024], BF16) as ptm0,
        n.sbuf_tensor("ptm1", [128, 1024], BF16) as ptm1,
        n.psum_tensor("psVQ0a", [128, 512], F32) as psVQ0a,
        n.psum_tensor("psVQ0b", [128, 512], F32) as psVQ0b,
        n.psum_tensor("psVQ0c", [128, 512], F32) as psVQ0c,
        n.psum_tensor("psVQ1a", [128, 512], F32) as psVQ1a,
        n.psum_tensor("psVQ1b", [128, 512], F32) as psVQ1b,
        n.psum_tensor("psVQ1c", [128, 512], F32) as psVQ1c,
        n.Block("p1bA") as blk,
    ):
        ptm = [ptm0, ptm1]
        psVQ = [[psVQ0a, psVQ0b, psVQ0c], [psVQ1a, psVQ1b, psVQ1c]]
        gmap = [(0, 0), (0, 132), (0, 264), (1, 0), (1, 132), (1, 264),
                (2, 0), (2, 132)]

        s_vm = n.alloc_semaphore("s_vm")
        s_fza = n.alloc_semaphore("s_fza")
        s_dv = n.alloc_semaphore("s_dv")
        s_pw = n.alloc_semaphore("s_pw")

        @blk.tensor
        def _(t):
            for i in range(NT):
                par = i % 2
                if i >= 2:
                    t.wait_ge(s_dv, i - 1)
                ins = None
                for gi in range(8):
                    ti, off = gmap[gi]
                    ins = t.matmul(
                        psVQ[par][ti][:, off:off + 132],
                        qfm[:, gi * SEXT + i * 128:gi * SEXT + (i + 1) * 128],
                        vkbd[:, gi * 132:(gi + 1) * 132],
                        start=(gi in (0, 3, 6)), stop=(gi in (2, 5, 7)),
                    )
                ins.then_inc(s_vm, 1)

        @blk.vector
        def _(v):
            fz = _Fence(v, s_fza)
            for i in range(NT):
                par = i % 2
                v.wait_ge(s_vm, i + 1)
                if i >= 2:
                    v.wait_ge(s_pw, 16 * (i - 1))
                for ti, ng in ((0, 3), (1, 3), (2, 2)):
                    fz(v.tensor_single_scalar(
                        pden[:, ti * 12:ti * 12 + ng * 4]
                        .rearrange("p (g h) -> p g h", h=4),
                        psVQ[par][ti][:, 0:ng * 132]
                        .rearrange("p (g h e) -> p (g h) e", h=4, e=33)
                        [:, :, 32:33].rearrange("p (g h) e -> p g (h e)", h=4),
                        EPS0, ALU.add))
                fz(v.reciprocal(pden[:, 32:64], pden[:, 0:32]))
                ins = None
                for ti, ng in ((0, 3), (1, 3), (2, 2)):
                    g0 = ti * 3
                    ins = v.tensor_tensor(
                        ptm[par][:, g0 * 128:(g0 + ng) * 128]
                        .rearrange("p (g h e) -> p g h e", h=4, e=32),
                        psVQ[par][ti][:, 0:ng * 132]
                        .rearrange("p (g h e) -> p g h e", h=4, e=33)
                        [:, :, :, 0:32],
                        pden[:, 32 + ti * 12:32 + ti * 12 + ng * 4]
                        .rearrange("p (g h) -> p g h", h=4)
                        [:, :, :, None].broadcast_to((128, ng, 4, 32)),
                        ALU.mult)
                ins.then_inc(s_dv, 1)

        @blk.sync
        def _(sy):
            for i in range(NT):
                sy.wait_ge(s_dv, i + 1)
                sy.dma_start(out=pin_bf[i * 128:(i + 1) * 128, :],
                             in_=ptm[i % 2][:, :]).then_inc(s_pw, 16)
            sy.wait_ge(s_pw, 16 * NT)

    freesems()
    qfm_g.__exit__(None, None, None)

    snapsems()
    with (
        n.sbuf_tensor("wtm0", [128, 1024], F32) as wtm0,
        n.sbuf_tensor("wtm1", [128, 1024], F32) as wtm1,
        n.sbuf_tensor("wdt0", [128, 1024], BF16) as wdt0,
        n.sbuf_tensor("wdt1", [128, 1024], BF16) as wdt1,
        n.sbuf_tensor("wqs", [128, 32], F32) as wqs,
        n.sbuf_tensor("wqs2", [128, 32], F32) as wqs2,
        n.sbuf_tensor("wqrec", [128, 32], F32) as wqrec,
        n.sbuf_tensor("wqu", [128, 1024], F32) as wqu,
        n.sbuf_tensor("wqau", [128, 1024], F32) as wqau,
        n.sbuf_tensor("wqsg", [128, 1024], F32) as wqsg,
        n.sbuf_tensor("wqv8", [128, 1024], FP8) as wqv8,
        n.Block("p1w") as blk,
    ):
        wtm = [wtm0, wtm1]
        wdt = [wdt0, wdt1]
        s_wl = n.alloc_semaphore("s_wl")
        s_fzw = n.alloc_semaphore("s_fzw")
        s_qa = n.alloc_semaphore("s_qa")
        s_qb = n.alloc_semaphore("s_qb")
        s_qc = n.alloc_semaphore("s_qc")
        s_ws = n.alloc_semaphore("s_ws")
        s_wq2 = n.alloc_semaphore("s_wq2")
        s_br = n.alloc_semaphore("s_br")

        @blk.sync
        def _(sy):
            for j in range(8):
                if j >= 2:
                    sy.wait_ge(s_qa, j - 1)
                sy.dma_start(out=wtm[j % 2][:, :],
                             in_=w_ffn[j * 128:(j + 1) * 128, :]
                             ).then_inc(s_wl, 16)
            sy.wait_ge(s_ws, 16 * 8)
            for kc in range(8):
                sy.dma_start_transpose(
                    wd_sb[:, kc * 1024:(kc + 1) * 1024],
                    wd_bf[0:1024, kc * 128:(kc + 1) * 128],
                ).then_inc(s_wq2, 16)
            for kc in range(8):
                sy.dma_start_transpose(
                    wp_sb[:, kc * 1024:(kc + 1) * 1024],
                    wproj_bf[0:1024, kc * 128:(kc + 1) * 128],
                ).then_inc(s_wq2, 16)
            sy.wait_ge(s_wq2, 16 * 16)
            sy.wait_ge(s_br, 32)
            sy.wait_ge(s_wl, 16 * 8)

        @blk.gpsimd
        def _(g):
            g.dma_start(out=bp_row[:, :], in_=b_proj).then_inc(s_br, 16)
            g.dma_start(out=bfn_row[:, :], in_=b_ffn).then_inc(s_br, 16)
            for j in range(8):
                g.wait_ge(s_qc, j + 1)
                g.dma_start(out=wd_bf[j * 128:(j + 1) * 128, :],
                            in_=wdt[j % 2][:, :]).then_inc(s_ws, 16)

        @blk.vector
        def _(v):
            fz = _Fence(v, s_fzw)
            for j in range(8):
                par = j % 2
                v.wait_ge(s_wl, 16 * (j + 1))
                _quant_dve_pre(v, wtm[par][:, :], wqs, wqs2, wqrec, wqu,
                               fz).then_inc(s_qa, 1)
                v.wait_ge(s_qb, j + 1)
                if j >= 2:
                    v.wait_ge(s_ws, 16 * (j - 1))
                _quant_dve_post(v, wqau, wqsg, wqu, wqv8, wqs2,
                                wdt[par][:, :]).then_inc(s_qc, 1)

        @blk.scalar
        def _(s):
            for j in range(8):
                s.wait_ge(s_qa, j + 1)
                s.activation(wqau[:, :], wqu[:, :], AF.Abs)
                s.activation(wqsg[:, :], wqu[:, :], AF.Sign).then_inc(s_qb, 1)



    freesems()
    NXW = 9
    from contextlib import ExitStack
    snapsems()
    with ExitStack() as _es:
        def _sb(name, shape, dt):
            return _es.enter_context(n.sbuf_tensor(name, shape, dt))

        def _ps(name):
            return _es.enter_context(n.psum_tensor(name, [128, 512], F32))

        xtm0 = _sb("xtm0", [128, 1024], F32)
        xtm1 = _sb("xtm1", [128, 1024], F32)
        pfm0 = _sb("pfm0", [128, 8 * 512], BF16)
        pfm1 = _sb("pfm1", [128, 8 * 512], BF16)
        dfm0 = _sb("dfm0", [128, 8 * 512], BF16)
        dfm1 = _sb("dfm1", [128, 8 * 512], BF16)
        xw_all = _sb("xw", [128, NXW * 1024], F32)
        xw20 = _sb("xw20", [128, 1024], F32)
        xw21 = _sb("xw21", [128, 1024], F32)
        xdt0 = _sb("xdt0", [128, 1024], BF16)
        xdt1 = _sb("xdt1", [128, 1024], BF16)
        lncol = _sb("lncol", [128, 16], F32)
        qqs = _sb("qqs", [128, 32], F32)
        qqs2 = _sb("qqs2", [128, 32], F32)
        qqrec = _sb("qqrec", [128, 32], F32)
        qqu = _sb("qqu", [128, 1024], F32)
        qqau = _sb("qqau", [128, 1024], F32)
        qqsg = _sb("qqsg", [128, 1024], F32)
        qqv8 = _sb("qqv8", [128, 1024], FP8)
        lnscr = _sb("lnscr", [128, 1024], F32)
        psP0 = _ps("psP0")
        psP1 = _ps("psP1")
        psF0 = _ps("psF0")
        psF1 = _ps("psF1")
        blk = _es.enter_context(n.Block("p1bB"))
        xtm = [xtm0, xtm1]
        pfm = [pfm0, pfm1]
        dfm = [dfm0, dfm1]
        xw = [xw_all[:, k * 1024:(k + 1) * 1024] for k in range(NXW)]
        xw2 = [xw20, xw21]
        xdt = [xdt0, xdt1]
        psP = [psP0, psP1]
        psF = [psF0, psF1]

        s_xl = n.alloc_semaphore("s_xl")
        s_pt = n.alloc_semaphore("s_pt")
        s_pm = n.alloc_semaphore("s_pm")
        s_c = n.alloc_semaphore("s_c")
        s_l1 = n.alloc_semaphore("s_l1")
        s_sq1 = n.alloc_semaphore("s_sq1")
        s_fzb = n.alloc_semaphore("s_fzb")
        s_sq2 = n.alloc_semaphore("s_sq2")
        s_l1r = n.alloc_semaphore("s_l1r")
        s_qa = n.alloc_semaphore("s_qa2")
        s_qb = n.alloc_semaphore("s_qb2")
        s_q = n.alloc_semaphore("s_q")
        s_xd = n.alloc_semaphore("s_xd")
        s_dt = n.alloc_semaphore("s_dt")
        s_fm = n.alloc_semaphore("s_fm")
        s_f = n.alloc_semaphore("s_f")
        s_l2 = n.alloc_semaphore("s_l2")
        s_l2r = n.alloc_semaphore("s_l2r")
        s_fin = n.alloc_semaphore("s_fin")
        s_x2 = n.alloc_semaphore("s_x2")
        s_cst2 = n.alloc_semaphore("s_cst2")

        def sup_of(i):
            return min(i // 4, 8)

        @blk.sync
        def _(sy):
            supend = {t0 + nt - 1: u for u, (t0, nt) in enumerate(SUPS)}
            supstart = {t0: u for u, (t0, nt) in enumerate(SUPS)}
            for i in range(NT + 8):
                j = i - 3
                if 0 <= j < NT:
                    sy.wait_ge(s_q, j + 1)
                    sy.dma_start(out=xd_bf[j * 128:(j + 1) * 128, :],
                                 in_=xdt[j % 2][:, :]).then_inc(s_xd, 16)
                if 0 <= j < NT and j in supend:
                    u = supend[j]
                    t0, nt = SUPS[u]
                    T = nt * 128
                    if u >= 2:
                        pt0, pnt = SUPS[u - 2]
                        sy.wait_ge(s_fm, 2 * (pt0 + pnt))
                    sy.wait_ge(s_xd, 16 * (t0 + nt))
                    for kc in range(8):
                        sy.dma_start_transpose(
                            dfm[u % 2][:, kc * 512:kc * 512 + T],
                            xd_bf[t0 * 128:t0 * 128 + T,
                                  kc * 128:(kc + 1) * 128],
                        ).then_inc(s_dt, 16)
                k = i - 8
                if 0 <= k < NT:
                    sy.wait_ge(s_fin, k + 1)
                    sy.dma_start(out=x2_tm[k * 128:(k + 1) * 128, :],
                                 in_=xw2[k % 2][:, :]).then_inc(s_x2, 16)
                if i < NT:
                    if i in supstart:
                        u = supstart[i]
                        t0, nt = SUPS[u]
                        T = nt * 128
                        if u >= 2:
                            pt0, pnt = SUPS[u - 2]
                            sy.wait_ge(s_pm, 2 * (pt0 + pnt))
                        for kc in range(8):
                            sy.dma_start_transpose(
                                pfm[u % 2][:, kc * 512:kc * 512 + T],
                                pin_bf[t0 * 128:t0 * 128 + T,
                                       kc * 128:(kc + 1) * 128],
                            ).then_inc(s_pt, 16)
                    if i >= 2:
                        sy.wait_ge(s_c, i - 1)
                    sy.dma_start(out=xtm[i % 2][:, :],
                                 in_=x_ext[i * 128:(i + 1) * 128, :]
                                 ).then_inc(s_xl, 16)
            sy.wait_ge(s_x2, 16 * NT)
            sy.wait_ge(s_xl, 16 * NT)
            sy.wait_ge(s_xd, 16 * NT)
            sy.wait_ge(s_pt, 16 * 8 * len(SUPS))
            sy.wait_ge(s_dt, 16 * 8 * len(SUPS))

        @blk.gpsimd
        def _(g):
            g.wait_ge(s_x2, 16 * NT)
            g.dma_start(out=x2_bf, in_=x2_tm).then_inc(s_cst2, 16)
            g.wait_ge(s_cst2, 16)

        @blk.tensor
        def _(t):
            def proj_mms(u):
                t0, nt = SUPS[u]
                t.wait_ge(s_pt, (u + 1) * 128)
                for k in range(nt):
                    i = t0 + k
                    if i >= 1:
                        t.wait_ge(s_c, i)
                    for oc in range(2):
                        ins = None
                        for kc in range(8):
                            ins = t.matmul(
                                psP[oc][:, :],
                                pfm[u % 2][:, kc * 512 + k * 128:
                                           kc * 512 + k * 128 + 128],
                                wp_sb[:, kc * 1024 + oc * 512:
                                      kc * 1024 + (oc + 1) * 512],
                                start=(kc == 0), stop=False,
                            )
                        ins = t.matmul(
                            psP[oc][:, :], onesr[:, 0:128],
                            bp_row[:, oc * 512:(oc + 1) * 512],
                            start=False, stop=True)
                        ins.then_inc(s_pm, 1)

            def ffn_mms(u):
                t0, nt = SUPS[u]
                t.wait_ge(s_dt, (u + 1) * 128)
                for k in range(nt):
                    i = t0 + k
                    if i >= 1:
                        t.wait_ge(s_f, i)
                    for oc in range(2):
                        ins = None
                        for kc in range(8):
                            ins = t.matmul(
                                psF[oc][:, :],
                                dfm[u % 2][:, kc * 512 + k * 128:
                                           kc * 512 + k * 128 + 128],
                                wd_sb[:, kc * 1024 + oc * 512:
                                      kc * 1024 + (oc + 1) * 512],
                                start=(kc == 0), stop=False,
                            )
                        ins = t.matmul(
                            psF[oc][:, :], onesr[:, 0:128],
                            bfn_row[:, oc * 512:(oc + 1) * 512],
                            start=False, stop=True)
                        ins.then_inc(s_fm, 1)

            proj_mms(0)
            for u in range(1, len(SUPS)):
                proj_mms(u)
                ffn_mms(u - 1)
            ffn_mms(len(SUPS) - 1)

        @blk.vector
        def _(v):
            fz = _Fence(v, s_fzb)

            def evacF_ln2(k):
                w = xw[k % NXW]
                v.wait_ge(s_fm, 2 * k + 2)
                if k >= 2:
                    v.wait_ge(s_x2, 16 * (k - 1))
                v.tensor_add(xw2[k % 2][:, 0:512], psF[0][:, :], w[:, 0:512])
                v.tensor_add(xw2[k % 2][:, 512:1024], psF[1][:, :],
                             w[:, 512:1024]).then_inc(s_f, 1)
                fz(v.tensor_scalar(xw2[k % 2][:, :], xw2[k % 2][:, :], 1.0,
                                   0.0, ALU.mult, ALU.add,
                                   accum_out=lncol[:, 8:9]))
                v.tensor_single_scalar(lncol[:, 8:9], lncol[:, 8:9], INV_C,
                                       ALU.mult)
                v.wait_ge(s_sq2, k + 1)
                v.tensor_single_scalar(lncol[:, 9:10], lncol[:, 9:10], INV_C,
                                       ALU.mult)
                fz(v.tensor_tensor(lncol[:, 10:11], lncol[:, 8:9],
                                   lncol[:, 8:9], ALU.mult))
                fz(v.tensor_tensor(lncol[:, 9:10], lncol[:, 9:10],
                                   lncol[:, 10:11], ALU.subtract))
                v.tensor_single_scalar(lncol[:, 9:10], lncol[:, 9:10], LN_EPS,
                                       ALU.add).then_inc(s_l2, 1)
                v.wait_ge(s_l2r, k + 1)
                v.tensor_scalar(xw2[k % 2][:, :], xw2[k % 2][:, :],
                                lncol[:, 8:9], lncol[:, 11:12],
                                ALU.subtract, ALU.mult).then_inc(s_fin, 1)

            def evacC_ln1_quant(i):
                w = xw[i % NXW]
                v.wait_ge(s_pm, 2 * i + 2)
                v.wait_ge(s_xl, 16 * (i + 1))
                if i >= NXW:
                    v.wait_ge(s_f, i - NXW + 1)
                v.tensor_add(w[:, 0:512], psP[0][:, :], xtm[i % 2][:, 0:512])
                v.tensor_add(w[:, 512:1024], psP[1][:, :],
                             xtm[i % 2][:, 512:1024]).then_inc(s_c, 1)
                fz(v.tensor_scalar(w[:, :], w[:, :], 1.0, 0.0, ALU.mult,
                                   ALU.add, accum_out=lncol[:, 0:1]))
                v.tensor_single_scalar(lncol[:, 0:1], lncol[:, 0:1], INV_C,
                                       ALU.mult)
                v.wait_ge(s_sq1, i + 1)
                v.tensor_single_scalar(lncol[:, 1:2], lncol[:, 1:2], INV_C,
                                       ALU.mult)
                fz(v.tensor_tensor(lncol[:, 2:3], lncol[:, 0:1],
                                   lncol[:, 0:1], ALU.mult))
                fz(v.tensor_tensor(lncol[:, 1:2], lncol[:, 1:2],
                                   lncol[:, 2:3], ALU.subtract))
                v.tensor_single_scalar(lncol[:, 1:2], lncol[:, 1:2], LN_EPS,
                                       ALU.add).then_inc(s_l1, 1)
                v.wait_ge(s_l1r, i + 1)
                v.tensor_scalar(w[:, :], w[:, :], lncol[:, 0:1],
                                lncol[:, 3:4], ALU.subtract, ALU.mult)
                _quant_dve_pre(v, w[:, :], qqs, qqs2, qqrec, qqu,
                               fz).then_inc(s_qa, 1)
                v.wait_ge(s_qb, i + 1)
                if i >= 2:
                    v.wait_ge(s_xd, 16 * (i - 1))
                _quant_dve_post(v, qqau, qqsg, qqu, qqv8, qqs2,
                                xdt[i % 2][:, :]).then_inc(s_q, 1)

            NU = len(SUPS)
            for u in range(NU + 2):
                if 2 <= u < NU + 2:
                    t0, nt = SUPS[u - 2]
                    for k in range(t0, t0 + nt):
                        evacF_ln2(k)
                if u < NU:
                    t0, nt = SUPS[u]
                    for i in range(t0, t0 + nt):
                        evacC_ln1_quant(i)

        @blk.scalar
        def _(s):
            NU = len(SUPS)
            for u in range(NU + 2):
                if 2 <= u < NU + 2:
                    t0, nt = SUPS[u - 2]
                    for k in range(t0, t0 + nt):
                        s.wait_ge(s_f, k + 1)
                        s.activation(lnscr[:, :], xw2[k % 2][:, :], AF.Square,
                                     accum_out=lncol[:, 9:10]
                                     ).then_inc(s_sq2, 1)
                        s.wait_ge(s_l2, k + 1)
                        s.activation(lncol[:, 11:12], lncol[:, 9:10], AF.Ln)
                        s.activation(lncol[:, 11:12], lncol[:, 11:12], AF.Exp,
                                     scale=-0.5).then_inc(s_l2r, 1)
                if u < NU:
                    t0, nt = SUPS[u]
                    for i in range(t0, t0 + nt):
                        s.wait_ge(s_c, i + 1)
                        s.activation(lnscr[:, :], xw[i % NXW][:, :], AF.Square,
                                     accum_out=lncol[:, 1:2]
                                     ).then_inc(s_sq1, 1)
                        s.wait_ge(s_l1, i + 1)
                        s.activation(lncol[:, 3:4], lncol[:, 1:2], AF.Ln)
                        s.activation(lncol[:, 3:4], lncol[:, 3:4], AF.Exp,
                                     scale=-0.5).then_inc(s_l1r, 1)
                        s.wait_ge(s_qa, i + 1)
                        s.activation(qqau[:, :], qqu[:, :], AF.Abs)
                        s.activation(qqsg[:, :], qqu[:, :],
                                     AF.Sign).then_inc(s_qb, 1)

    for cx in reversed(wctx):
        cx.__exit__(None, None, None)
    freesems()

    if BUILD_UPTO == 2:
        return nc

    def vgi(st, k, og):
        return st * 8 + k * 2 + og

    def qst_T2(st):
        return 512 if st < 8 else SEXT - 8 * 512

    snapsems()
    with ExitStack() as _es:
        def _sb(name, shape, dt):
            return _es.enter_context(n.sbuf_tensor(name, shape, dt))

        xfm20 = _sb("xfm20", [128, 8 * 512], BF16)
        xfm21 = _sb("xfm21", [128, 8 * 512], BF16)
        wq1 = _sb("wq1", [128, 8 * 1024], BF16)
        wk1 = _sb("wk1", [128, 8 * 1024], BF16)
        wv1 = _sb("wv1", [128, 8 * 1024], BF16)
        bqc_sb = _sb("bqc_sb", [128, 8], F32)
        bkc_sb = _sb("bkc_sb", [128, 8], F32)
        bqs_sb = _sb("bqs_sb", [128, 8], F32)
        bv_row = _sb("bv_row", [1, 1024], F32R)
        qt_sb = _sb("qt_sb", [128, 8 * 512], BF16)
        kt_sb = _sb("kt_sb", [128, 8 * 512], BF16)
        vt0 = _sb("vt0", [128, 1040], BF16)
        vt1 = _sb("vt1", [128, 1040], BF16)
        psQ0 = _es.enter_context(n.psum_tensor("ps2q0", [128, 512], F32))
        psQ1 = _es.enter_context(n.psum_tensor("ps2q1", [128, 512], F32))
        psK0 = _es.enter_context(n.psum_tensor("ps2k0", [128, 512], F32))
        psK1 = _es.enter_context(n.psum_tensor("ps2k1", [128, 512], F32))
        psV0 = _es.enter_context(n.psum_tensor("ps2v0", [128, 512], F32))
        psV1 = _es.enter_context(n.psum_tensor("ps2v1", [128, 512], F32))
        blk = _es.enter_context(n.Block("p2qkv"))

        xfm2 = [xfm20, xfm21]
        qkt = [qt_sb, kt_sb]
        vt = [vt0, vt1]
        psQK = [[psQ0, psQ1], [psK0, psK1]]
        psV = [psV0, psV1]

        c_wl = n.alloc_semaphore("c_wl")
        c_bq = n.alloc_semaphore("c_bq")
        c_xt = n.alloc_semaphore("c_xt")
        c_qkm = n.alloc_semaphore("c_qkm")
        c_qke = n.alloc_semaphore("c_qke")
        c_qks = n.alloc_semaphore("c_qks")
        c_vm = n.alloc_semaphore("c_vm")
        c_ve = n.alloc_semaphore("c_ve")
        c_vs = n.alloc_semaphore("c_vs")

        @blk.gpsimd
        def _(g):
            g.dma_start(out=wq1[:, :].rearrange("p (kc o) -> p kc o", kc=8),
                        in_=wq_i.rearrange("(kc p) o -> p kc o", p=128)
                        ).then_inc(c_wl, 16)
            g.dma_start(out=wk1[:, :].rearrange("p (kc o) -> p kc o", kc=8),
                        in_=wk_i.rearrange("(kc p) o -> p kc o", p=128)
                        ).then_inc(c_wl, 16)
            g.dma_start(out=wv1[:, :].rearrange("p (kc o) -> p kc o", kc=8),
                        in_=wv_i.rearrange("(kc p) o -> p kc o", p=128)
                        ).then_inc(c_wl, 16)
            g.dma_start(out=bv_row[:, :], in_=bv_i).then_inc(c_wl, 16)
            g.dma_start(out=bqc_sb[:, :], in_=bqc_i).then_inc(c_wl, 16)
            g.dma_start(out=bkc_sb[:, :], in_=bkc_i).then_inc(c_wl, 16)
            for st in range(NST_EXT):
                nt = (qst_T2(st) + 127) // 128
                for k in range(nt):
                    i = st * 4 + k
                    g.wait_ge(c_ve, vgi(st, k, 1) + 1)
                    g.dma_start(out=vpad_d[i * 128:(i + 1) * 128, :],
                                in_=vt[i % 2][:, :]).then_inc(c_vs, 16)
            g.wait_ge(c_vs, 16 * NT)

        @blk.sync
        def _(sy):
            for st in range(NST_EXT):
                T = qst_T2(st)
                if st >= 2:
                    sy.wait_ge(c_vm, vgi(st - 2, (qst_T2(st - 2) + 127)
                                         // 128 - 1, 1) + 1)
                for kc in range(8):
                    sy.dma_start_transpose(
                        xfm2[st % 2][:, kc * 512:kc * 512 + T],
                        x2_bf[st * 512:st * 512 + T,
                              kc * 128:(kc + 1) * 128],
                    ).then_inc(c_xt, 16)
                for w in range(2):
                    sy.wait_ge(c_qke, (st * 2 + w) * 8 + 8)
                    dst = qfm_d if w == 0 else kfm_d
                    sy.dma_start(
                        out=dst.rearrange("(oc p) t -> p oc t", p=128)
                        [:, :, st * 512:st * 512 + T],
                        in_=qkt[w][:, :].rearrange("p (oc t) -> p oc t", oc=8)
                        [:, :, 0:T],
                    ).then_inc(c_qks, 16)
            sy.wait_ge(c_qks, 16 * NST_EXT * 2)
            sy.wait_ge(c_xt, 16 * 8 * NST_EXT)
            sy.wait_ge(c_wl, 16 * 6)

        @blk.tensor
        def _(t):
            t.wait_ge(c_wl, 16 * 6)
            for st in range(NST_EXT):
                T = qst_T2(st)
                nt = (T + 127) // 128
                t.wait_ge(c_xt, 16 * 8 * (st + 1))
                for w in range(2):
                    wsb = wq1 if w == 0 else wk1
                    for oc in range(8):
                        gidx = (st * 2 + w) * 8 + oc
                        if oc >= 2:
                            t.wait_ge(c_qke, gidx - 1)
                        elif st >= 1:
                            t.wait_ge(c_qks, 16 * ((st - 1) * 2 + w + 1))
                        ins = None
                        for kc in range(8):
                            ins = t.matmul(
                                psQK[w][oc % 2][:, 0:T],
                                wsb[:, kc * 1024 + oc * 128:
                                    kc * 1024 + (oc + 1) * 128],
                                xfm2[st % 2][:, kc * 512:kc * 512 + T],
                                start=(kc == 0), stop=(kc == 7),
                            )
                        ins.then_inc(c_qkm, 1)
                for k in range(nt):
                    for og in range(2):
                        vg = vgi(st, k, og)
                        if vg >= 2:
                            t.wait_ge(c_ve, vg - 1)
                        ins = None
                        for kc in range(8):
                            ins = t.matmul(
                                psV[og % 2][:, :],
                                xfm2[st % 2][:, kc * 512 + k * 128:
                                             kc * 512 + k * 128 + 128],
                                wv1[:, kc * 1024 + og * 512:
                                    kc * 1024 + (og + 1) * 512],
                                start=(kc == 0), stop=False,
                            )
                        ins = t.matmul(
                            psV[og % 2][:, :], onesr[:, 0:128],
                            bv_row[:, og * 512:(og + 1) * 512],
                            start=False, stop=True)
                        ins.then_inc(c_vm, 1)

        @blk.scalar
        def _(s):
            for st in range(NST_EXT):
                T = qst_T2(st)
                nt = (T + 127) // 128
                for w in range(2):
                    for oc in range(8):
                        gidx = (st * 2 + w) * 8 + oc
                        s.wait_ge(c_qkm, gidx + 1)
                        if st >= 1 and oc == 0:
                            s.wait_ge(c_qks, 16 * ((st - 1) * 2 + w + 1))
                        if st == 0 and w == 0 and oc == 0:
                            s.wait_ge(c_bq, 1)
                        if w == 0:
                            s.activation(
                                qkt[w][:, oc * 512:oc * 512 + T],
                                psQK[w][oc % 2][:, 0:T], AF.Identity,
                                scale=0.125,
                                bias=bqs_sb[:, oc:oc + 1]).then_inc(c_qke, 1)
                        else:
                            s.activation(
                                qkt[w][:, oc * 512:oc * 512 + T],
                                psQK[w][oc % 2][:, 0:T], AF.Identity,
                                bias=bkc_sb[:, oc:oc + 1]).then_inc(c_qke, 1)
                for k in range(nt):
                    i = st * 4 + k
                    for og in range(2):
                        vg = vgi(st, k, og)
                        s.wait_ge(c_vm, vg + 1)
                        if og == 0 and i >= 2:
                            s.wait_ge(c_vs, 16 * (i - 1))
                        s.activation(
                            vt[i % 2][:, :]
                            .rearrange("p (h e) -> p h e", e=65)
                            [:, og * 8:(og + 1) * 8, 0:64],
                            psV[og % 2][:, :]
                            .rearrange("p (h d) -> p h d", d=64),
                            AF.Copy,
                        ).then_inc(c_ve, 1)

        @blk.vector
        def _(v):
            for par in range(2):
                v.memset(
                    vt[par][:, :].rearrange("p (h e) -> p h e", e=65)
                    [:, :, 64:65], 1.0)
            v.wait_ge(c_wl, 16 * 6)
            v.tensor_single_scalar(bqs_sb[:, :], bqc_sb[:, :], 0.125,
                                   ALU.mult).then_inc(c_bq, 1)

    freesems()
    if BUILD_UPTO == 3:
        return nc

    NQC = SOWN // 128
    snapsems()
    with ExitStack() as _es:
        def _sb(name, shape, dt):
            return _es.enter_context(n.sbuf_tensor(name, shape, dt))

        qwin0 = _sb("qwin0", [128, 8 * 512], BF16)
        qwin1 = _sb("qwin1", [128, 8 * 512], BF16)
        kwin0 = _sb("kwin0", [128, 8 * 768], BF16)
        kwin1 = _sb("kwin1", [128, 8 * 768], BF16)
        vwin0 = _sb("vwin0", [128, 6 * 1040], BF16)
        vwin1 = _sb("vwin1", [128, 6 * 1040], BF16)
        biasT_sb = _sb("biasT_sb", [128, H1 * 3 * 128], F32)
        ebias_sb = _sb("ebias_sb", [128, H1 * 3 * 128], BF16)
        ebL_sb = _sb("ebL_sb", [128, H1 * 128], BF16)
        ebR_sb = _sb("ebR_sb", [128, H1 * 128], BF16)
        ebsc = _sb("ebsc", [128, H1 * 128], F32)
        scb0 = _sb("scb0", [128, 384], F32)
        scb1 = _sb("scb1", [128, 384], F32)
        exq0 = _sb("exq0", [128, 384], BF16)
        exq1 = _sb("exq1", [128, 384], BF16)
        exb0 = _sb("exb0", [128, 384], BF16)
        exb1 = _sb("exb1", [128, 384], BF16)
        nrmc = _sb("nrmc", [128, 16], F32)
        att0 = _sb("att0", [128, 1024], BF16)
        att1 = _sb("att1", [128, 1024], BF16)
        psS0 = _es.enter_context(n.psum_tensor("psS0", [128, 512], F32))
        psS1 = _es.enter_context(n.psum_tensor("psS1", [128, 512], F32))
        psAV = _es.enter_context(n.psum_tensor("psAV", [128, 2048], F32))
        blk = _es.enter_context(n.Block("p2attn"))

        qwin = [qwin0, qwin1]
        kwin = [kwin0, kwin1]
        vwin = [vwin0, vwin1]
        scb = [scb0, scb1]
        exq = [exq0, exq1]
        exb = [exb0, exb1]
        att = [att0, att1]
        psS = [psS0, psS1]

        a_cn = n.alloc_semaphore("a_cn")
        a_fz = n.alloc_semaphore("a_fz")
        a_ebl = n.alloc_semaphore("a_ebl")
        a_cl = n.alloc_semaphore("a_cl")
        a_ebx = n.alloc_semaphore("a_ebx")
        a_qw = n.alloc_semaphore("a_qw")
        a_kw = n.alloc_semaphore("a_kw")
        a_vw = n.alloc_semaphore("a_vw")
        a_sc = n.alloc_semaphore("a_sc")
        a_ex = n.alloc_semaphore("a_ex")
        a_eb = n.alloc_semaphore("a_eb")
        a_av = n.alloc_semaphore("a_av")
        a_nm = n.alloc_semaphore("a_nm")
        a_st = n.alloc_semaphore("a_st")

        @blk.gpsimd
        def _(g):
            g.dma_start(out=biasT_sb[:, :]
                        .rearrange("p (h r q) -> p h r q", h=H1, r=3),
                        in_=biasT_i.rearrange("h r w q -> w h r q")
                        ).then_inc(a_cn, 16)
            g.dma_start(out=ebsc[:, :].rearrange("p (h q) -> p h q", h=H1),
                        in_=biasTL_i.rearrange("h w q -> w h q")
                        ).then_inc(a_ebl, 16)
            g.wait_ge(a_ebx, 2)
            g.dma_start(out=ebsc[:, :].rearrange("p (h q) -> p h q", h=H1),
                        in_=biasTR_i.rearrange("h w q -> w h q")
                        ).then_inc(a_ebl, 16)

        @blk.sync
        def _(sy):
            for step in range(NQC + 2):
                qc = step - 2
                if qc >= 0:
                    sy.wait_ge(a_nm, qc + 1)
                    sy.dma_start(out=attn_bf[qc * 128:(qc + 1) * 128, :],
                                 in_=att[qc % 2][:, :]).then_inc(a_st, 16)
                if step < NQC and step % 4 == 0:
                    qb = step // 4
                    if qb >= 2:
                        sy.wait_ge(a_nm, 4 * (qb - 1))
                    for oc in range(8):
                        sy.dma_start(
                            out=qwin[qb % 2][:, oc * 512:(oc + 1) * 512],
                            in_=qfm_d[oc * 128:(oc + 1) * 128,
                                      128 + 512 * qb:128 + 512 * qb + 512],
                        ).then_inc(a_qw, 16)
                    for oc in range(8):
                        sy.dma_start(
                            out=kwin[qb % 2][:, oc * 768:(oc + 1) * 768],
                            in_=kfm_d[oc * 128:(oc + 1) * 128,
                                      512 * qb:512 * qb + 768],
                        ).then_inc(a_kw, 16)
                    for kv in range(6):
                        sy.dma_start(
                            out=vwin[qb % 2][:, kv * 1040:(kv + 1) * 1040],
                            in_=vpad_d[512 * qb + kv * 128:
                                       512 * qb + (kv + 1) * 128, :],
                        ).then_inc(a_vw, 16)
            sy.wait_ge(a_st, 16 * NQC)
            sy.wait_ge(a_qw, 16 * 64)
            sy.wait_ge(a_kw, 16 * 64)
            sy.wait_ge(a_vw, 16 * 48)

        @blk.tensor
        def _(t):

            def sc_mms(qc, h):
                qb = qc // 4
                m = qc - 4 * qb
                sidx = qc * 16 + h
                if sidx >= 2:
                    t.wait_ge(a_cl, sidx - 1)
                pb = psS[h % 2]
                hb = 64 * (h % 2)
                hc = h // 2
                ins = None
                for r in range(3):
                    ins = t.matmul(
                        pb[:, 128 * r:128 * (r + 1)],
                        kwin[qb % 2][hb:hb + 64,
                                     hc * 768 + (m + r) * 128:
                                     hc * 768 + (m + r + 1) * 128],
                        qwin[qb % 2][hb:hb + 64,
                                     hc * 512 + m * 128:
                                     hc * 512 + (m + 1) * 128],
                        start=(r == 0), stop=(r == 2))
                ins.then_inc(a_sc, 1)

            def av_mms(qc, h):
                qb = qc // 4
                m = qc - 4 * qb
                sidx = qc * 16 + h
                t.wait_ge(a_eb, sidx + 1)
                if h == 0 and qc >= 1:
                    t.wait_ge(a_nm, qc)
                ins = None
                for r in range(3):
                    ins = t.matmul(
                        psAV[:, 128 * h:128 * h + 65],
                        exb[h % 2][:, 128 * r:128 * (r + 1)],
                        vwin[qb % 2][:, (m + r) * 1040 + 65 * h:
                                     (m + r) * 1040 + 65 * (h + 1)],
                        start=(r == 0 and h % 4 == 0),
                        stop=(r == 2 and h % 4 == 3),
                    )
                ins.then_inc(a_av, 1)

            for qc in range(NQC):
                qb = qc // 4
                t.wait_ge(a_qw, 16 * 8 * (qb + 1))
                t.wait_ge(a_kw, 16 * 8 * (qb + 1))
                t.wait_ge(a_vw, 16 * 6 * (qb + 1))
                sc_mms(qc, 0)
                sc_mms(qc, 1)
                for h in range(16):
                    if h + 2 < 16:
                        sc_mms(qc, h + 2)
                    av_mms(qc, h)

        @blk.vector
        def _(v):
            fz = _Fence(v, a_fz)
            v.wait_ge(a_ebx, 3)

            def eb_mult(qc, h):
                sidx = qc * 16 + h
                v.wait_ge(a_ex, sidx + 1)
                if sidx >= 2:
                    v.wait_ge(a_av, sidx - 1)
                e = exb[h % 2]
                q = exq[h % 2]
                if qc == 0:
                    v.tensor_tensor(e[:, 0:128], q[:, 0:128],
                                    ebL_sb[:, h * 128:(h + 1) * 128],
                                    ALU.mult)
                    v.tensor_tensor(e[:, 128:384], q[:, 128:384],
                                    ebias_sb[:, h * 384 + 128:
                                             h * 384 + 384],
                                    ALU.mult).then_inc(a_eb, 1)
                elif qc == NQC - 1:
                    v.tensor_tensor(e[:, 0:256], q[:, 0:256],
                                    ebias_sb[:, h * 384:h * 384 + 256],
                                    ALU.mult)
                    v.tensor_tensor(e[:, 256:384], q[:, 256:384],
                                    ebR_sb[:, h * 128:(h + 1) * 128],
                                    ALU.mult).then_inc(a_eb, 1)
                else:
                    v.tensor_tensor(e[:, :], q[:, :],
                                    ebias_sb[:, h * 384:(h + 1) * 384],
                                    ALU.mult).then_inc(a_eb, 1)

            for qc in range(NQC):
                for h in range(16):
                    sidx = qc * 16 + h
                    v.wait_ge(a_sc, sidx + 1)
                    if sidx >= 2:
                        v.wait_ge(a_ex, sidx - 1)
                    v.tensor_single_scalar(
                        scb[h % 2][:, :], psS[h % 2][:, 0:384],
                        8.0 * CLAMP, ALU.min).then_inc(a_cl, 1)
                    if h >= 1:
                        eb_mult(qc, h - 1)
                eb_mult(qc, 15)
                v.wait_ge(a_av, 16 * (qc + 1))
                if qc >= 2:
                    v.wait_ge(a_st, 16 * (qc - 1))
                fz(v.reciprocal(
                    nrmc[:, :],
                    psAV[:, :].rearrange("p (h e) -> p h e", e=128)
                    [:, :, 64:65].rearrange("p h e -> p (h e)")))
                v.tensor_tensor(
                    att[qc % 2][:, :].rearrange("p (h d) -> p h d", d=64),
                    psAV[:, :].rearrange("p (h e) -> p h e", e=128)
                    [:, :, 0:64],
                    nrmc[:, :].rearrange("p (h o) -> p h o", o=1)
                    .broadcast_to((128, 16, 64)),
                    ALU.mult).then_inc(a_nm, 1)

        @blk.scalar
        def _(s):
            s.wait_ge(a_cn, 16)
            s.activation(ebias_sb[:, :], biasT_sb[:, :],
                         AF.Exp).then_inc(a_ebx, 1)
            s.wait_ge(a_ebl, 16)
            s.activation(ebL_sb[:, :], ebsc[:, :],
                         AF.Exp).then_inc(a_ebx, 1)
            s.wait_ge(a_ebl, 32)
            s.activation(ebR_sb[:, :], ebsc[:, :],
                         AF.Exp).then_inc(a_ebx, 1)
            for qc in range(NQC):
                for h in range(16):
                    sidx = qc * 16 + h
                    s.wait_ge(a_cl, sidx + 1)
                    if sidx >= 2:
                        s.wait_ge(a_eb, sidx - 1)
                    s.activation(exq[h % 2][:, :], scb[h % 2][:, :],
                                 AF.Exp, scale=0.125).then_inc(a_ex, 1)

    freesems()
    if BUILD_UPTO == 4:
        return nc

    NSP = SOWN // 512
    snapsems()
    with ExitStack() as _es:
        def _sb(name, shape, dt):
            return _es.enter_context(n.sbuf_tensor(name, shape, dt))

        afm0 = _sb("afm0", [128, 8 * 512], BF16)
        afm1 = _sb("afm1", [128, 8 * 512], BF16)
        wo1 = _sb("wo1", [128, 8 * 1024], BF16)
        bo_row = _sb("bo_row", [1, 1024], F32R)
        x2r0 = _sb("x2r0", [128, 1024], F32)
        x2r1 = _sb("x2r1", [128, 1024], F32)
        x3t0 = _sb("x3t0", [128, 1024], F32)
        x3t1 = _sb("x3t1", [128, 1024], F32)
        ln3 = _sb("ln3", [128, 8], F32)
        ln3s = _sb("ln3s", [128, 1024], F32)
        psO0 = _es.enter_context(n.psum_tensor("psO0", [128, 512], F32))
        psO1 = _es.enter_context(n.psum_tensor("psO1", [128, 512], F32))
        blk = _es.enter_context(n.Block("p2proj"))

        afm = [afm0, afm1]
        x2r = [x2r0, x2r1]
        x3t = [x3t0, x3t1]
        psO = [psO0, psO1]

        o_wl = n.alloc_semaphore("o_wl")
        o_at = n.alloc_semaphore("o_at")
        o_xl = n.alloc_semaphore("o_xl")
        o_pm = n.alloc_semaphore("o_pm")
        o_ev = n.alloc_semaphore("o_ev")
        o_l3 = n.alloc_semaphore("o_l3")
        o_sq = n.alloc_semaphore("o_sq")
        o_fz = n.alloc_semaphore("o_fz")
        o_l3r = n.alloc_semaphore("o_l3r")
        o_fin = n.alloc_semaphore("o_fin")
        o_st = n.alloc_semaphore("o_st")
        o_cst = n.alloc_semaphore("o_cst")

        NTO = SOWN // 128

        @blk.gpsimd
        def _(g):
            g.dma_start(out=wo1[:, :].rearrange("p (kc o) -> p kc o", kc=8),
                        in_=wo_i.rearrange("(kc p) o -> p kc o", p=128)
                        ).then_inc(o_wl, 16)
            g.dma_start(out=bo_row[:, :], in_=bo_i).then_inc(o_wl, 16)
            g.wait_ge(o_st, 16 * NTO)
            g.dma_start(out=x3_bf, in_=x3_tm).then_inc(o_cst, 16)
            g.wait_ge(o_cst, 16)

        @blk.sync
        def _(sy):
            for i in range(NTO + 4):
                k = i - 4
                if 0 <= k < NTO:
                    sy.wait_ge(o_fin, k + 1)
                    sy.dma_start(out=x3_tm[k * 128:(k + 1) * 128, :],
                                 in_=x3t[k % 2][:, :]).then_inc(o_st, 16)
                if i < NTO:
                    if i % 4 == 0:
                        u = i // 4
                        if u >= 2:
                            sy.wait_ge(o_pm, 2 * (u - 1) * 4)
                        for kc in range(8):
                            sy.dma_start_transpose(
                                afm[u % 2][:, kc * 512:(kc + 1) * 512],
                                attn_bf[u * 512:(u + 1) * 512,
                                        kc * 128:(kc + 1) * 128],
                            ).then_inc(o_at, 16)
                    if i >= 2:
                        sy.wait_ge(o_ev, i - 1)
                    sy.dma_start(out=x2r[i % 2][:, :],
                                 in_=x2_tm[(i + 1) * 128:(i + 2) * 128, :]
                                 ).then_inc(o_xl, 16)
            sy.wait_ge(o_st, 16 * NTO)
            sy.wait_ge(o_at, 16 * 8 * NSP)
            sy.wait_ge(o_xl, 16 * NTO)
            sy.wait_ge(o_wl, 32)

        @blk.tensor
        def _(t):
            t.wait_ge(o_wl, 32)
            for i in range(NTO):
                u = i // 4
                k = i % 4
                if k == 0:
                    t.wait_ge(o_at, 16 * 8 * (u + 1))
                if i >= 1:
                    t.wait_ge(o_ev, i)
                for oc in range(2):
                    ins = None
                    for kc in range(8):
                        ins = t.matmul(
                            psO[oc][:, :],
                            afm[u % 2][:, kc * 512 + k * 128:
                                       kc * 512 + k * 128 + 128],
                            wo1[:, kc * 1024 + oc * 512:
                                kc * 1024 + (oc + 1) * 512],
                            start=(kc == 0), stop=False,
                        )
                    ins = t.matmul(
                        psO[oc][:, :], onesr[:, 0:128],
                        bo_row[:, oc * 512:(oc + 1) * 512],
                        start=False, stop=True)
                    ins.then_inc(o_pm, 1)

        @blk.vector
        def _(v):
            fz = _Fence(v, o_fz)
            for i in range(NTO):
                v.wait_ge(o_pm, 2 * i + 2)
                v.wait_ge(o_xl, 16 * (i + 1))
                if i >= 2:
                    v.wait_ge(o_st, 16 * (i - 1))
                v.tensor_add(x3t[i % 2][:, 0:512], psO[0][:, :],
                             x2r[i % 2][:, 0:512])
                v.tensor_add(x3t[i % 2][:, 512:1024], psO[1][:, :],
                             x2r[i % 2][:, 512:1024]).then_inc(o_ev, 1)
                fz(v.tensor_scalar(x3t[i % 2][:, :], x3t[i % 2][:, :], 1.0,
                                   0.0, ALU.mult, ALU.add,
                                   accum_out=ln3[:, 0:1]))
                v.tensor_single_scalar(ln3[:, 0:1], ln3[:, 0:1], INV_C,
                                       ALU.mult)
                v.wait_ge(o_sq, i + 1)
                v.tensor_single_scalar(ln3[:, 1:2], ln3[:, 1:2], INV_C,
                                       ALU.mult)
                fz(v.tensor_tensor(ln3[:, 2:3], ln3[:, 0:1], ln3[:, 0:1],
                                   ALU.mult))
                fz(v.tensor_tensor(ln3[:, 1:2], ln3[:, 1:2], ln3[:, 2:3],
                                   ALU.subtract))
                v.tensor_single_scalar(ln3[:, 1:2], ln3[:, 1:2], LN_EPS,
                                       ALU.add).then_inc(o_l3, 1)
                v.wait_ge(o_l3r, i + 1)
                v.tensor_scalar(x3t[i % 2][:, :], x3t[i % 2][:, :],
                                ln3[:, 0:1], ln3[:, 3:4],
                                ALU.subtract, ALU.mult).then_inc(o_fin, 1)

        @blk.scalar
        def _(s):
            for i in range(NTO):
                s.wait_ge(o_ev, i + 1)
                s.activation(ln3s[:, :], x3t[i % 2][:, :], AF.Square,
                             accum_out=ln3[:, 1:2]).then_inc(o_sq, 1)
                s.wait_ge(o_l3, i + 1)
                s.activation(ln3[:, 3:4], ln3[:, 1:2], AF.Ln)
                s.activation(ln3[:, 3:4], ln3[:, 3:4], AF.Exp,
                             scale=-0.5).then_inc(o_l3r, 1)

    freesems()
    if BUILD_UPTO == 5:
        return nc

    NHALF = 2
    HTOK = SOWN // 2
    NTC = HTOK // 512
    NTT = HTOK // 128
    BIGM = 1e30

    snapsems()
    with ExitStack() as _es:
        def _sb(name, shape, dt):
            return _es.enter_context(n.sbuf_tensor(name, shape, dt))

        x3fm = _sb("x3fm", [128, NTC * 8 * 512], BF16)
        w1s = _sb("w1s", [128, 8 * 2048], BF16)
        w2s0 = _sb("w2s0", [128, 16 * 1024], BF16)
        w2s1 = _sb("w2s1", [128, 16 * 1024], BF16)
        yhalf = _sb("yhalf", [128, NTT * 1024], BF16)
        hm0 = _sb("hm0", [128, 16 * 512], BF16)
        ohT_sb = _sb("ohT_sb", [8, HTOK], BF16)
        maxp_sb = _sb("maxp_sb", [128, NTT], F32)
        b2sb = _sb("b2sb", [8, 1024], BF16)
        br_row = _sb("br_row", [1, 8], F32R)
        b1col = _sb("b1col", [128, 8 * 16], F32)
        wr_sb = _sb("wr_sb", [128, 8 * 8], BF16)
        idsb3 = _sb("idsb3", [128, 128], BF16)
        esel = _sb("esel", [8, 8 * 128], BF16)
        onesb = _sb("onesb", [1, 512], BF16)
        lgt = _sb("lgt", [128, 8], F32)
        rxp = _sb("rxp", [128, 8], F32)
        rcol = _sb("rcol", [128, 4], F32)
        ohsb = _sb("ohsb", [128, 8], BF16)
        mkb0 = _sb("mkb0", [128, 512], BF16)
        x3r0 = _sb("x3r0", [128, 1024], F32)
        x3r1 = _sb("x3r1", [128, 1024], F32)
        x4t0 = _sb("x4t0", [128, 1024], F32)
        x4t1 = _sb("x4t1", [128, 1024], F32)
        ln4 = _sb("ln4", [128, 8], F32)
        psR = _es.enter_context(n.psum_tensor("psR", [128, 512], F32))
        psRT = _es.enter_context(n.psum_tensor("psRT", [8, 128], BF16))
        psH0 = _es.enter_context(n.psum_tensor("psH3", [128, 512], F32))
        psH1 = _es.enter_context(n.psum_tensor("psH4", [128, 512], F32))
        psM = _es.enter_context(n.psum_tensor("psM", [128, 512], F32))
        psY0 = _es.enter_context(n.psum_tensor("psY3", [128, 512], F32))
        psY1 = _es.enter_context(n.psum_tensor("psY4", [128, 512], F32))
        blk = _es.enter_context(n.Block("p3"))

        hm = [hm0, hm0]
        mkb = [mkb0, mkb0]
        x3r = [x3r0, x3r1]
        x4t = [x4t0, x4t1]
        w2s = [w2s0, w2s1]
        psH = [psH0, psH1]
        psY = [psY0, psY1]

        m_cn = n.alloc_semaphore("m_cn")
        m_cs = n.alloc_semaphore("m_cs")
        m_w1 = n.alloc_semaphore("m_w1")
        m_w2 = n.alloc_semaphore("m_w2")
        m_xt = n.alloc_semaphore("m_xt")
        m_rt = n.alloc_semaphore("m_rt")
        m_rx = n.alloc_semaphore("m_rx")
        m_ra = n.alloc_semaphore("m_ra")
        m_ro = n.alloc_semaphore("m_ro")
        m_rt2 = n.alloc_semaphore("m_rt2")
        m_re = n.alloc_semaphore("m_re")
        m_mk = n.alloc_semaphore("m_mk")
        m_mke = n.alloc_semaphore("m_mke")
        m_hm = n.alloc_semaphore("m_hm")
        m_he = n.alloc_semaphore("m_he")
        m_hme = n.alloc_semaphore("m_hme")
        m_ym = n.alloc_semaphore("m_ym")
        m_ye = n.alloc_semaphore("m_ye")
        m_xl = n.alloc_semaphore("m_xl")
        m_fin = n.alloc_semaphore("m_fin")
        m_l4 = n.alloc_semaphore("m_l4")
        m_x4 = n.alloc_semaphore("m_x4")
        m_sq4 = n.alloc_semaphore("m_sq4")
        m_l4r = n.alloc_semaphore("m_l4r")
        m_st = n.alloc_semaphore("m_st")
        m_fz = n.alloc_semaphore("m_fz")

        def rtile(half, r):
            return half * NTT + r

        def ygrp(half, e, tc, g):
            return ((half * 8 + e) * NTC + tc) * 8 + g

        def hgrp(half, e, tc, hc):
            return ((half * 8 + e) * NTC + tc) * 16 + hc

        def mkidx(half, e, tc):
            return (half * 8 + e) * NTC + tc

        @blk.gpsimd
        def _(g):
            g.dma_start(out=b2sb[:, :], in_=b2_i).then_inc(m_cn, 16)
            g.dma_start(out=br_row[:, :], in_=br_i).then_inc(m_cn, 16)
            g.dma_start(out=wr_sb[:, :].rearrange("p (kc o) -> p kc o", kc=8),
                        in_=wr_i.rearrange("(kc p) o -> p kc o", p=128)
                        ).then_inc(m_cn, 16)
            for li in range(16):
                half, e = divmod(li, 8)
                if li >= 1:
                    ph, pe = divmod(li - 1, 8)
                    g.wait_ge(m_hm, hgrp(ph, pe, NTC - 1, 15) + 1)
                g.dma_start(
                    out=w1s[:, :].rearrange("p (kc h) -> p kc h", kc=8),
                    in_=w1_i[e].rearrange("(kc p) h -> p kc h", p=128),
                ).then_inc(m_w1, 16)
                if li >= 2:
                    ph, pe = divmod(li - 2, 8)
                    g.wait_ge(m_ym, ygrp(ph, pe, NTC - 1, 7) + 1)
                g.dma_start(
                    out=w2s[li % 2][:, :].rearrange("p (hc o) -> p hc o",
                                                    hc=16),
                    in_=w2_i[e].rearrange("(hc p) o -> p hc o", p=128),
                ).then_inc(m_w2, 16)
            g.wait_ge(m_w1, 16 * 16)
            g.wait_ge(m_w2, 16 * 16)
            g.wait_ge(m_cn, 48)

        @blk.sync
        def _(sy):
            sy.dma_start(out=idsb3[:, :], in_=identb).then_inc(m_cs, 16)
            sy.dma_start(out=b1col[:, :], in_=b1_col).then_inc(m_cs, 16)
            sy.dma_start(out=esel[:, :], in_=esel_i).then_inc(m_cs, 16)
            for half in range(NHALF):
                if half == 1:
                    sy.wait_ge(m_hm, hgrp(0, 7, NTC - 1, 15) + 1)
                for tc in range(NTC):
                    for kc in range(8):
                        sy.dma_start_transpose(
                            x3fm[:, (tc * 8 + kc) * 512:
                                 (tc * 8 + kc + 1) * 512],
                            x3_bf[half * HTOK + tc * 512:
                                  half * HTOK + (tc + 1) * 512,
                                  kc * 128:(kc + 1) * 128],
                        ).then_inc(m_xt, 16)
                for i in range(NTT):
                    fi = half * NTT + i
                    if fi >= 2:
                        sy.wait_ge(m_sq4, fi - 1)
                    sy.dma_start(
                        out=x3r[fi % 2][:, :],
                        in_=x3_tm[half * HTOK + i * 128:
                                  half * HTOK + (i + 1) * 128, :]
                    ).then_inc(m_xl, 16)
                    if fi >= 2:
                        kf = fi - 2
                        sy.wait_ge(m_fin, kf + 1)
                        sy.dma_start(
                            out=out_c[(kf // NTT) * HTOK + (kf % NTT) * 128:
                                      (kf // NTT) * HTOK + (kf % NTT) * 128
                                      + 128, :],
                            in_=x4t[kf % 2][:, :]).then_inc(m_st, 16)
            for kf in range(NHALF * NTT - 2, NHALF * NTT):
                sy.wait_ge(m_fin, kf + 1)
                sy.dma_start(
                    out=out_c[(kf // NTT) * HTOK + (kf % NTT) * 128:
                              (kf // NTT) * HTOK + (kf % NTT) * 128 + 128, :],
                    in_=x4t[kf % 2][:, :]).then_inc(m_st, 16)
            sy.wait_ge(m_st, 16 * NHALF * NTT)
            sy.wait_ge(m_xt, 16 * NHALF * NTC * 8)
            sy.wait_ge(m_xl, 16 * NHALF * NTT)
            sy.wait_ge(m_cs, 48)

        @blk.tensor
        def _(t):
            t.wait_ge(m_cn, 48)
            t.wait_ge(m_cs, 48)
            for half in range(NHALF):
                t.wait_ge(m_xt, 16 * (half + 1) * NTC * 8)
                for r in range(NTT):
                    ri = rtile(half, r)
                    tc, k = r // 4, r % 4
                    if ri >= 2:
                        t.wait_ge(m_rx, ri - 1)
                    ins = None
                    for kc in range(8):
                        ins = t.matmul(
                            psR[:, (ri % 2) * 8:(ri % 2) * 8 + 8],
                            x3fm[:, (tc * 8 + kc) * 512 + k * 128:
                                 (tc * 8 + kc) * 512 + (k + 1) * 128],
                            wr_sb[:, kc * 8:(kc + 1) * 8],
                            start=(kc == 0), stop=False)
                    ins = t.matmul(
                        psR[:, (ri % 2) * 8:(ri % 2) * 8 + 8],
                        onesr[:, 0:128], br_row[:, :],
                        start=False, stop=True)
                    ins.then_inc(m_rt, 1)
                    if r >= 1:
                        t.wait_ge(m_ro, ri)
                        if ri >= 2:
                            t.wait_ge(m_re, ri - 1)
                        t.transpose(psRT[:, :], ohsb[:, :],
                                    idsb3[:, :]).then_inc(m_rt2, 1)
                t.wait_ge(m_ro, rtile(half, NTT - 1) + 1)
                t.wait_ge(m_re, rtile(half, NTT - 1))
                t.transpose(psRT[:, :], ohsb[:, :],
                            idsb3[:, :]).then_inc(m_rt2, 1)
                for e in range(8):
                    li = half * 8 + e
                    if e == 0:
                        t.wait_ge(m_re, (half + 1) * NTT)
                    t.wait_ge(m_w1, 16 * (li + 1))
                    for tc in range(NTC):
                        mi = mkidx(half, e, tc)
                        if mi >= 2:
                            t.wait_ge(m_mke, mi)
                        t.matmul(psM[:, :],
                                 esel[:, e * 128:(e + 1) * 128],
                                 ohT_sb[:, tc * 512:(tc + 1) * 512],
                                 start=True, stop=True).then_inc(m_mk, 1)
                        for hc in range(16):
                            hg = hgrp(half, e, tc, hc)
                            if hg >= 2:
                                t.wait_ge(m_he, hg - 1)
                            ins = None
                            for kc in range(8):
                                ins = t.matmul(
                                    psH[hc % 2][:, :],
                                    w1s[:, kc * 2048 + hc * 128:
                                        kc * 2048 + (hc + 1) * 128],
                                    x3fm[:, (tc * 8 + kc) * 512:
                                         (tc * 8 + kc + 1) * 512],
                                    start=(kc == 0), stop=(kc == 7),
                                )
                            ins.then_inc(m_hm, 1)
                        t.wait_ge(m_w2, 16 * (li + 1))
                        t.wait_ge(m_hme, hgrp(half, e, tc, 15) + 1)
                        for ts_ in range(4):
                            for oc in range(2):
                                gg = ts_ * 2 + oc
                                yg = ygrp(half, e, tc, gg)
                                if yg >= 2:
                                    t.wait_ge(m_ye, yg - 1)
                                first = (e == 0)
                                ins = None
                                if first:
                                    ins = t.matmul(
                                        psY[gg % 2][:, :],
                                        ohT_sb[:, tc * 512 + ts_ * 128:
                                               tc * 512 + (ts_ + 1) * 128],
                                        b2sb[:, oc * 512:(oc + 1) * 512],
                                        start=True, stop=False)
                                for hc in range(16):
                                    ins = t.matmul(
                                        psY[gg % 2][:, :],
                                        hm[tc % 2][:, hc * 512 + ts_ * 128:
                                                   hc * 512 + (ts_ + 1) * 128],
                                        w2s[li % 2][:, hc * 1024 + oc * 512:
                                                    hc * 1024 + (oc + 1) * 512],
                                        start=(hc == 0 and not first),
                                        stop=(hc == 15),
                                    )
                                ins.then_inc(m_ym, 1)

        @blk.scalar
        def _(s):
            def ln4sA(half, i):
                fi = half * NTT + i
                b4 = 4 * (i % 2)
                s.wait_ge(m_x4, fi + 1)
                s.activation(x3r[fi % 2][:, :],
                             x4t[i % 2][:, :], AF.Square,
                             accum_out=ln4[:, b4 + 1:b4 + 2]
                             ).then_inc(m_sq4, 1)

            def ln4sB(half, i):
                fi = half * NTT + i
                b4 = 4 * (i % 2)
                s.wait_ge(m_l4, fi + 1)
                s.activation(ln4[:, b4 + 3:b4 + 4],
                             ln4[:, b4 + 1:b4 + 2], AF.Ln)
                s.activation(ln4[:, b4 + 3:b4 + 4],
                             ln4[:, b4 + 3:b4 + 4], AF.Exp,
                             scale=-0.5).then_inc(m_l4r, 1)

            def ln4s(half, tc4):
                t0 = 4 * tc4
                ln4sA(half, t0)
                ln4sA(half, t0 + 1)
                ln4sB(half, t0)
                ln4sA(half, t0 + 2)
                ln4sB(half, t0 + 1)
                ln4sA(half, t0 + 3)
                ln4sB(half, t0 + 2)
                ln4sB(half, t0 + 3)

            for half in range(NHALF):
                for r in range(NTT):
                    ri = rtile(half, r)
                    s.wait_ge(m_rx, ri + 1)
                    s.activation(rxp[:, :], lgt[:, :], AF.Exp,
                                 bias=rcol[:, 1:2],
                                 accum_out=rcol[:, 2:3]).then_inc(m_ra, 1)
                for e in range(8):
                    for tc in range(NTC):
                        for hc in range(16):
                            hg = hgrp(half, e, tc, hc)
                            s.wait_ge(m_hm, hg + 1)
                            mi = mkidx(half, e, tc)
                            if mi >= 1 and hc == 0:
                                s.wait_ge(m_ym, mi * 8)
                            s.activation(
                                hm[tc % 2][:, hc * 512:(hc + 1) * 512],
                                psH[hc % 2][:, :], AF.Relu,
                                bias=b1col[:, e * 16 + hc:e * 16 + hc + 1]
                            ).then_inc(m_he, 1)
                        if e == 7 and tc >= 2:
                            ln4s(half, tc - 2)
                ln4s(half, 2)
                ln4s(half, 3)

        @blk.vector
        def _(v):
            fz = _Fence(v, m_fz)

            def ln4vA(half, i):
                fi = half * NTT + i
                b4 = 4 * (i % 2)
                v.wait_ge(m_xl, 16 * (fi + 1))
                if fi >= 2:
                    v.wait_ge(m_st, 16 * (fi - 1))
                ysl = yhalf[:, i * 1024:(i + 1) * 1024]
                v.tensor_single_scalar(ysl, ysl, maxp_sb[:, i:i + 1],
                                       ALU.mult)
                v.tensor_add(x4t[i % 2][:, :], ysl, x3r[fi % 2][:, :])
                v.tensor_scalar(x4t[i % 2][:, :], x4t[i % 2][:, :],
                                1.0, 0.0, ALU.mult, ALU.add,
                                accum_out=ln4[:, b4:b4 + 1]
                                ).then_inc(m_x4, 1)

            def ln4vB(half, i):
                fi = half * NTT + i
                b4 = 4 * (i % 2)
                v.wait_ge(m_x4, fi + 1)
                v.tensor_single_scalar(ln4[:, b4:b4 + 1],
                                       ln4[:, b4:b4 + 1], INV_C, ALU.mult)
                v.wait_ge(m_sq4, fi + 1)
                v.tensor_single_scalar(ln4[:, b4 + 1:b4 + 2],
                                       ln4[:, b4 + 1:b4 + 2], INV_C,
                                       ALU.mult)
                fz(v.tensor_tensor(ln4[:, b4 + 2:b4 + 3], ln4[:, b4:b4 + 1],
                                   ln4[:, b4:b4 + 1], ALU.mult))
                fz(v.tensor_tensor(ln4[:, b4 + 1:b4 + 2],
                                   ln4[:, b4 + 1:b4 + 2],
                                   ln4[:, b4 + 2:b4 + 3], ALU.subtract))
                v.tensor_single_scalar(ln4[:, b4 + 1:b4 + 2],
                                       ln4[:, b4 + 1:b4 + 2], LN_EPS,
                                       ALU.add).then_inc(m_l4, 1)
                v.wait_ge(m_l4r, fi + 1)
                v.tensor_scalar(x4t[i % 2][:, :], x4t[i % 2][:, :],
                                ln4[:, b4:b4 + 1], ln4[:, b4 + 3:b4 + 4],
                                ALU.subtract, ALU.mult).then_inc(m_fin, 1)

            def ln4v(half, tc4):
                t0 = 4 * tc4
                ln4vA(half, t0)
                ln4vA(half, t0 + 1)
                ln4vB(half, t0)
                ln4vA(half, t0 + 2)
                ln4vB(half, t0 + 1)
                ln4vA(half, t0 + 3)
                ln4vB(half, t0 + 2)
                ln4vB(half, t0 + 3)

            v.memset(onesb[:, :], 1.0)
            for half in range(NHALF):
                for r in range(NTT):
                    ri = rtile(half, r)
                    v.wait_ge(m_rt, ri + 1)
                    if half == 1 and r == 0:
                        v.wait_ge(m_ye, ygrp(0, 7, NTC - 1, 7) + 1)
                    if ri >= 1:
                        v.wait_ge(m_ra, ri)
                        v.wait_ge(m_rt2, ri)
                    fz(v.tensor_copy(lgt[:, :],
                                     psR[:, (ri % 2) * 8:(ri % 2) * 8 + 8]))
                    fz(v.reduce_max(rcol[:, 0:1], lgt[:, :], axis=AX.X))
                    v.tensor_single_scalar(rcol[:, 1:2], rcol[:, 0:1], -1.0,
                                           ALU.mult).then_inc(m_rx, 1)
                    v.wait_ge(m_ra, ri + 1)
                    fz(v.reciprocal(rcol[:, 3:4], rcol[:, 2:3]))
                    v.tensor_copy(maxp_sb[:, r:r + 1], rcol[:, 3:4])
                    v.tensor_single_scalar(ohsb[:, :], lgt[:, :],
                                           rcol[:, 0:1],
                                           ALU.is_ge).then_inc(m_ro, 1)
                    v.wait_ge(m_rt2, ri + 1)
                    v.tensor_copy(ohT_sb[:, r * 128:(r + 1) * 128],
                                  psRT[:, :]).then_inc(m_re, 1)
                for e in range(8):
                    for tc in range(NTC):
                        mi = mkidx(half, e, tc)
                        v.wait_ge(m_mk, mi + 1)
                        if mi >= 2:
                            pj = mi - 2
                            phe, ptc = divmod(pj, NTC)
                            v.wait_ge(m_ym, (phe * NTC + ptc) * 8 + 8)
                        v.tensor_single_scalar(mkb[mi % 2][:, :], psM[:, :],
                                               BIGM,
                                               ALU.mult).then_inc(m_mke, 1)
                        for hc in range(16):
                            hg = hgrp(half, e, tc, hc)
                            v.wait_ge(m_he, hg + 1)
                            v.tensor_tensor(
                                hm[tc % 2][:, hc * 512:(hc + 1) * 512],
                                hm[tc % 2][:, hc * 512:(hc + 1) * 512],
                                mkb[mi % 2][:, :],
                                ALU.min).then_inc(m_hme, 1)
                        for gg in range(8):
                            ts_, oc = gg // 2, gg % 2
                            yg = ygrp(half, e, tc, gg)
                            v.wait_ge(m_ym, yg + 1)
                            ysl = yhalf[:, (tc * 4 + ts_) * 1024 + oc * 512:
                                        (tc * 4 + ts_) * 1024 +
                                        (oc + 1) * 512]
                            if e == 0:
                                v.tensor_copy(ysl, psY[gg % 2][:, :]
                                              ).then_inc(m_ye, 1)
                            else:
                                v.tensor_add(ysl, ysl, psY[gg % 2][:, :]
                                             ).then_inc(m_ye, 1)
                        if e == 7 and tc >= 2:
                            ln4v(half, tc - 2)
                ln4v(half, 2)
                ln4v(half, 3)

    return nc



def _host_consts(inputs, core):
    b, h = core // 2, core % 2
    rpb = np.asarray(inputs["L1_attn_rel_pos_bias"], np.float32)
    w = np.arange(128)[:, None]
    q = np.arange(128)[None, :]
    biasT = np.empty((H1, 3, 128, 128), np.float32)
    for r in range(3):
        delta = 128 * r + w - 128 - q
        idx = np.clip(delta + 128, 0, 256)
        valid = (delta >= -128) & (delta <= 128)
        for hh in range(H1):
            biasT[hh, r] = np.where(valid, rpb[hh][idx], NEG)
    biasTL = biasT[:, 0].copy()
    biasTR = biasT[:, 2].copy()
    if h == 0:
        biasTL[:] = NEG
    else:
        biasTR[:] = NEG
    return {
        "biasT_i": np.ascontiguousarray(biasT),
        "biasTL_i": np.ascontiguousarray(biasTL),
        "biasTR_i": np.ascontiguousarray(biasTR),
        "identb": np.eye(128, dtype=np.float32).astype(BF),
    }


def _core_inputs(inputs, core):
    b, h = core // 2, core % 2
    x = np.ascontiguousarray(inputs["L0_attn_x"][b])
    lo = h * SOWN - MARG
    idx = (np.arange(lo, lo + SEXT)) % S
    x_ext = np.ascontiguousarray(x[idx])
    d = {
        "x_all": x,
        "x_ext": x_ext,
        "w_qkv": np.ascontiguousarray(inputs["L0_attn_qkv_weight"]),
        "w_proj": np.ascontiguousarray(inputs["L0_attn_proj_weight"]),
        "b_proj": np.ascontiguousarray(inputs["L0_attn_proj_bias"])
        .reshape(1, C),
        "w_ffn": np.ascontiguousarray(inputs["L0_ffn_weight"]),
        "b_ffn": np.ascontiguousarray(inputs["L0_ffn_bias"]).reshape(1, C),
        "ones128": np.ones((1, 512), np.float32),
    }
    if BUILD_UPTO >= 3:
        d.update({
            "wq_i": np.ascontiguousarray(inputs["L1_attn_Wq"]),
            "wk_i": np.ascontiguousarray(inputs["L1_attn_Wk"]),
            "wv_i": np.ascontiguousarray(inputs["L1_attn_Wv"]),
            "bv_i": np.ascontiguousarray(inputs["L1_attn_bv"]).reshape(1, C),
            "bqc_i": np.ascontiguousarray(
                np.asarray(inputs["L1_attn_bq"]).reshape(8, 128).T),
            "bkc_i": np.ascontiguousarray(
                np.asarray(inputs["L1_attn_bk"]).reshape(8, 128).T),
        })
    if BUILD_UPTO >= 4:
        d.update(_host_consts(inputs, core))
    if BUILD_UPTO >= 5:
        d.update({
            "wo_i": np.ascontiguousarray(inputs["L1_attn_Wo"]),
            "bo_i": np.ascontiguousarray(inputs["L1_attn_bo"]).reshape(1, C),
        })
    if BUILD_UPTO >= 6:
        d.update({
            "wr_i": np.ascontiguousarray(inputs["L1_ffn_Wr"]),
            "br_i": np.ascontiguousarray(inputs["L1_ffn_br"]).reshape(1, 8),
            "w1_i": np.ascontiguousarray(inputs["L1_ffn_W1"]),
            "b1_col": np.ascontiguousarray(
                np.asarray(inputs["L1_ffn_b1"]).reshape(8, 16, 128)
                .transpose(2, 0, 1).reshape(128, 128)),
            "w2_i": np.ascontiguousarray(inputs["L1_ffn_W2"]),
            "b2_i": np.ascontiguousarray(inputs["L1_ffn_b2"]),
            "esel_i": np.ascontiguousarray(
                np.eye(8, dtype=np.float32)[:, :, None]
                .repeat(128, axis=2).reshape(8, 8 * 128).astype(BF)),
        })
    return d


def kernel(**inputs):
    nc = build_program()
    in_maps = [_core_inputs(inputs, c) for c in range(8)]
    res = run_bass_kernel_spmd(nc, in_maps, core_ids=list(range(8)))
    if BUILD_UPTO < 6:
        return res
    out = np.empty((B, S, C), np.float32)
    for c in range(8):
        b, h = c // 2, c % 2
        out[b, h * SOWN:(h + 1) * SOWN, :] = res.results[c]["out_c"]
    return out


# version 15
# speedup vs baseline: 1.0456x; 1.0456x over previous
import numpy as np
import ml_dtypes
import concourse.bass as bass
import concourse.mybir as mybir
from concourse.bass_utils import run_bass_kernel_spmd

F32 = mybir.dt.float32
F32R = mybir.dt.float32r
BF16 = mybir.dt.bfloat16
FP8 = mybir.dt.float8e4
AF = mybir.ActivationFunctionType
ALU = mybir.AluOpType
AX = mybir.AxisListType
BF = ml_dtypes.bfloat16

B, S, C = 4, 8192, 1024
SOWN = 4096
MARG = 128
SEXT = SOWN + 2 * MARG
NT = SEXT // 128
NST_ALL = S // 512
NST_EXT = 9
SUPS = [(u * 4, 4) for u in range(8)] + [(32, 2)]
H0, D0 = 32, 32
H1, D1 = 16, 64
EPS0 = 1e-15
LN_EPS = 1e-5
CLAMP = 80.0
NEG = -1e30
INV_C = 1.0 / C

BUILD_UPTO = 6
DEBUG_TAPS = False


class _Fence:

    def __init__(self, eng, sem):
        self.eng = eng
        self.sem = sem
        self.n = 0

    def __call__(self, ins):
        self.n += 1
        ins.then_inc(self.sem, 1)
        self.eng.wait_ge(self.sem, self.n)
        return ins


def _quant_dve_pre(v, src, qs, qs2, qrec, qu, fence):
    fence(v.reduce_max(qs[:, :], src.rearrange("p (b i) -> p b i", i=32),
                       axis=AX.X, apply_absolute_value=True))
    fence(v.tensor_scalar(qs2[:, :], qs[:, :], 1.0 / 7.5, 1e-30, ALU.mult,
                          ALU.max))
    fence(v.reciprocal(qrec[:, :], qs2[:, :]))
    return v.tensor_tensor(
        qu.rearrange("p (b i) -> p b i", i=32),
        src.rearrange("p (b i) -> p b i", i=32),
        qrec[:, :].rearrange("p (b o) -> p b o", o=1).broadcast_to((128, 32, 32)),
        ALU.mult)


def _quant_dve_post(v, qau, qsg, qu, qv8, qs2, dst):
    v.tensor_single_scalar(qau[:, :], qau[:, :], 1.0, ALU.is_lt)
    v.tensor_mul(qsg[:, :], qsg[:, :], qau[:, :])
    v.tensor_add(qu[:, :], qu[:, :], qsg[:, :])
    v.tensor_copy(qv8[:, :], qu[:, :])
    v.tensor_tensor(qu[:, :], qv8[:, :], qsg[:, :], ALU.subtract)
    return v.tensor_tensor(
        dst.rearrange("p (b i) -> p b i", i=32),
        qu.rearrange("p (b i) -> p b i", i=32),
        qs2[:, :].rearrange("p (b o) -> p b o", o=1).broadcast_to((128, 32, 32)),
        ALU.mult)


def build_program():
    nc = bass.Bass("TRN2", target_bir_lowering=False,
                   detect_race_conditions=False)
    n = nc
    dbg = "ExternalOutput" if DEBUG_TAPS else "Internal"

    x_all = n.dram_tensor("x_all", [S, C], F32, kind="ExternalInput").ap()
    x_ext = n.dram_tensor("x_ext", [SEXT, C], F32, kind="ExternalInput").ap()
    w_qkv = n.dram_tensor("w_qkv", [3 * C, C], F32, kind="ExternalInput").ap()
    w_proj = n.dram_tensor("w_proj", [C, C], F32, kind="ExternalInput").ap()
    b_proj = n.dram_tensor("b_proj", [1, C], F32, kind="ExternalInput").ap()
    w_ffn = n.dram_tensor("w_ffn", [C, C], F32, kind="ExternalInput").ap()
    b_ffn = n.dram_tensor("b_ffn", [1, C], F32, kind="ExternalInput").ap()
    ones128 = n.dram_tensor("ones128", [1, 512], F32, kind="ExternalInput").ap()

    out_c = n.dram_tensor("out_c", [SOWN, C], F32, kind="ExternalOutput").ap()

    xbf_all = n.dram_tensor("xbf_all", [S, C], BF16, kind="Internal").ap()
    xbf_ext = n.dram_tensor("xbf_ext", [SEXT, C], BF16, kind="Internal").ap()
    wqkv_bf = n.dram_tensor("wqkv_bf", [3 * C, C], BF16, kind="Internal").ap()
    wproj_bf = n.dram_tensor("wproj_bf", [C, C], BF16, kind="Internal").ap()
    wd_bf = n.dram_tensor("wd_bf", [C, C], BF16, kind="Internal").ap()
    pin_bf = n.dram_tensor("pin_bf", [SEXT, C], BF16, kind=dbg).ap()
    xd_bf = n.dram_tensor("xd_bf", [SEXT, C], BF16, kind=dbg).ap()
    x2_tm = n.dram_tensor("x2_tm", [SEXT, C], F32, kind=dbg).ap()
    x2_bf = n.dram_tensor("x2_bf", [SEXT, C], BF16, kind="Internal").ap()

    vk_tap = n.dram_tensor("vk_tap", [128, 8 * 132], BF16, kind=dbg).ap()

    if BUILD_UPTO >= 3:
        wq_i = n.dram_tensor("wq_i", [C, C], F32, kind="ExternalInput").ap()
        wk_i = n.dram_tensor("wk_i", [C, C], F32, kind="ExternalInput").ap()
        wv_i = n.dram_tensor("wv_i", [C, C], F32, kind="ExternalInput").ap()
        bv_i = n.dram_tensor("bv_i", [1, C], F32, kind="ExternalInput").ap()
        bqc_i = n.dram_tensor("bqc_i", [128, 8], F32, kind="ExternalInput").ap()
        bkc_i = n.dram_tensor("bkc_i", [128, 8], F32, kind="ExternalInput").ap()
        qfm_d = n.dram_tensor("qfm_d", [C, SEXT], BF16, kind=dbg).ap()
        kfm_d = n.dram_tensor("kfm_d", [C, SEXT], BF16, kind=dbg).ap()
        vpad_d = n.dram_tensor("vpad_d", [SEXT, H1 * (D1 + 1)], BF16,
                               kind=dbg).ap()
    if BUILD_UPTO >= 4:
        biasT_i = n.dram_tensor("biasT_i", [H1, 3, 128, 128], F32,
                                kind="ExternalInput").ap()
        biasTL_i = n.dram_tensor("biasTL_i", [H1, 128, 128], F32,
                                 kind="ExternalInput").ap()
        biasTR_i = n.dram_tensor("biasTR_i", [H1, 128, 128], F32,
                                 kind="ExternalInput").ap()
        identb = n.dram_tensor("identb", [128, 128], BF16,
                               kind="ExternalInput").ap()
        attn_bf = n.dram_tensor("attn_bf", [SOWN, C], BF16, kind=dbg).ap()
    if BUILD_UPTO >= 5:
        wo_i = n.dram_tensor("wo_i", [C, C], F32, kind="ExternalInput").ap()
        bo_i = n.dram_tensor("bo_i", [1, C], F32, kind="ExternalInput").ap()
        x3_tm = n.dram_tensor("x3_tm", [SOWN, C], F32, kind=dbg).ap()
        x3_bf = n.dram_tensor("x3_bf", [SOWN, C], BF16, kind="Internal").ap()
    if BUILD_UPTO >= 6:
        wr_i = n.dram_tensor("wr_i", [C, 8], F32, kind="ExternalInput").ap()
        br_i = n.dram_tensor("br_i", [1, 8], F32, kind="ExternalInput").ap()
        w1_i = n.dram_tensor("w1_i", [8, C, 2048], F32,
                             kind="ExternalInput").ap()
        b1_col = n.dram_tensor("b1_col", [128, 8 * 16], F32,
                               kind="ExternalInput").ap()
        w2_i = n.dram_tensor("w2_i", [8, 2048, C], F32,
                             kind="ExternalInput").ap()
        b2_i = n.dram_tensor("b2_i", [8, C], F32, kind="ExternalInput").ap()
        esel_i = n.dram_tensor("esel_i", [8, 8 * 128], BF16,
                               kind="ExternalInput").ap()

    vkbd = n.alloc_sbuf_tensor("vkbd", [128, 8 * 132], BF16)
    onesr = n.alloc_sbuf_tensor("onesr", [1, 512], F32R)

    _semsnap = [None]

    def snapsems():
        _semsnap[0] = n.snapshot_sems()

    def freesems():
        n.clear_and_free_semaphores(n.allocated_since(_semsnap[0]))

    wctx = [
        n.sbuf_tensor("wp_sb", [128, 8 * 1024], BF16),
        n.sbuf_tensor("wd_sb", [128, 8 * 1024], BF16),
        n.sbuf_tensor("bp_row", [1, 1024], F32R),
        n.sbuf_tensor("bfn_row", [1, 1024], F32R),
    ]
    wp_sb, wd_sb, bp_row, bfn_row = [c.__enter__() for c in wctx]

    qfm_g = n.sbuf_tensor("qfm", [128, 8 * SEXT], BF16)
    qfm = qfm_g.__enter__()

    snapsems()
    with (
        n.sbuf_tensor("wkv_sb", [128, 8 * 2048], BF16) as wkv_sb,
        n.sbuf_tensor("wq_sb", [128, 8 * 1024], BF16) as wq_sb,
        n.sbuf_tensor("xfm0", [128, 8 * 512], BF16) as xfm0,
        n.sbuf_tensor("xfm1", [128, 8 * 512], BF16) as xfm1,
        n.sbuf_tensor("ktm0", [128, 1024], BF16) as ktm0,
        n.sbuf_tensor("ktm1", [128, 1024], BF16) as ktm1,
        n.sbuf_tensor("vpad0", [128, 1056], BF16) as vpad0,
        n.sbuf_tensor("vpad1", [128, 1056], BF16) as vpad1,
        n.psum_tensor("psKV0", [128, 512], F32) as psKV0,
        n.psum_tensor("psKV1", [128, 512], F32) as psKV1,
        n.psum_tensor("psVKa", [128, 512], F32) as psVKa,
        n.psum_tensor("psVKb", [128, 512], F32) as psVKb,
        n.psum_tensor("psVKc", [128, 512], F32) as psVKc,
        n.psum_tensor("psQ0", [128, 512], F32) as psQ0,
        n.psum_tensor("psQ1", [128, 512], F32) as psQ1,
        n.Block("p1a") as blk,
    ):
        xfm = [xfm0, xfm1]
        ktm = [ktm0, ktm1]
        vpad = [vpad0, vpad1]
        psKV = [psKV0, psKV1]
        psQ = [psQ0, psQ1]
        vk_ps = [(psVKa, 0), (psVKa, 132), (psVKa, 264),
                 (psVKb, 0), (psVKb, 132), (psVKb, 264),
                 (psVKc, 0), (psVKc, 132)]

        s_cast = n.alloc_semaphore("s_cast")
        s_xc = n.alloc_semaphore("s_xc")
        s_wt = n.alloc_semaphore("s_wt")
        s_xt = n.alloc_semaphore("s_xt")
        s_kv = n.alloc_semaphore("s_kv")
        s_ev = n.alloc_semaphore("s_ev")
        s_vk = n.alloc_semaphore("s_vk")
        s_qm = n.alloc_semaphore("s_qm")
        s_qe = n.alloc_semaphore("s_qe")
        s_st = n.alloc_semaphore("s_st")

        def qst_T(qst):
            return 512 if qst < 8 else SEXT - 8 * 512

        @blk.gpsimd
        def _(g):
            g.dma_start(out=wqkv_bf[1024:3072, :],
                        in_=w_qkv[1024:3072, :]).then_inc(s_cast, 16)
            for ch in range(4):
                if ch >= 1:
                    g.wait_ge(s_xt, 128 * (4 * ch - 3))
                g.dma_start(out=xbf_all[ch * 2048:(ch + 1) * 2048, :],
                            in_=x_all[ch * 2048:(ch + 1) * 2048, :]
                            ).then_inc(s_xc, 16)
            g.wait_ge(s_xt, 128 * 14)
            g.dma_start(out=xbf_ext, in_=x_ext).then_inc(s_cast, 16)
            g.dma_start(out=wqkv_bf[0:1024, :],
                        in_=w_qkv[0:1024, :]).then_inc(s_cast, 16)
            g.dma_start(out=onesr[:, :], in_=ones128).then_inc(s_cast, 16)
            g.dma_start(out=wproj_bf, in_=w_proj).then_inc(s_cast, 16)

        @blk.sync
        def _(sy):
            sy.wait_ge(s_cast, 16)
            for kc in range(8):
                sy.dma_start_transpose(
                    wkv_sb[:, kc * 2048:(kc + 1) * 2048],
                    wqkv_bf[1024:3072, kc * 128:(kc + 1) * 128],
                ).then_inc(s_wt, 16)
            for st in range(NST_ALL):
                if st >= 2:
                    sy.wait_ge(s_vk, (st - 2) * 4 + 4)
                sy.wait_ge(s_xc, 16 * (st // 4 + 1))
                for kc in range(8):
                    sy.dma_start_transpose(
                        xfm[st % 2][:, kc * 512:(kc + 1) * 512],
                        xbf_all[st * 512:(st + 1) * 512,
                                kc * 128:(kc + 1) * 128],
                    ).then_inc(s_xt, 16)
            sy.wait_ge(s_cast, 48)
            for kc in range(8):
                sy.dma_start_transpose(
                    wq_sb[:, kc * 1024:(kc + 1) * 1024],
                    wqkv_bf[0:1024, kc * 128:(kc + 1) * 128],
                ).then_inc(s_wt, 16)
            for qst in range(NST_EXT):
                T = qst_T(qst)
                if qst < 2:
                    sy.wait_ge(s_vk, NST_ALL * 4)
                else:
                    sy.wait_ge(s_qm, (qst - 2) * 8 + 8)
                for kc in range(8):
                    sy.dma_start_transpose(
                        xfm[qst % 2][:, kc * 512:kc * 512 + T],
                        xbf_ext[qst * 512:qst * 512 + T,
                                kc * 128:(kc + 1) * 128],
                    ).then_inc(s_xt, 16)
            if DEBUG_TAPS:
                sy.wait_ge(s_vk, NST_ALL * 4 + 1)
                sy.dma_start(out=vk_tap, in_=vkbd[:, :]).then_inc(s_st, 16)
                sy.wait_ge(s_st, 16)
            sy.wait_ge(s_xt, 16 * 8 * (NST_ALL + NST_EXT))
            sy.wait_ge(s_wt, 16 * 16)
            sy.wait_ge(s_cast, 16 * 5)
            sy.wait_ge(s_xc, 16 * 4)

        @blk.tensor
        def _(t):
            t.wait_ge(s_wt, 16 * 8)
            for st in range(NST_ALL):
                t.wait_ge(s_xt, (st + 1) * 128)
                for tsub in range(4):
                    kvg0 = (st * 4 + tsub) * 4
                    for og in range(4):
                        kvg = kvg0 + og
                        if kvg >= 2:
                            t.wait_ge(s_ev, kvg - 1)
                        ins = None
                        for kc in range(8):
                            ins = t.matmul(
                                psKV[og % 2][:, :],
                                xfm[st % 2][:, kc * 512 + tsub * 128:
                                            kc * 512 + tsub * 128 + 128],
                                wkv_sb[:, kc * 2048 + og * 512:
                                       kc * 2048 + (og + 1) * 512],
                                start=(kc == 0), stop=(kc == 7),
                            )
                        ins.then_inc(s_kv, 1)
                    t.wait_ge(s_ev, kvg0 + 4)
                    par = tsub % 2
                    first = (st == 0 and tsub == 0)
                    last = (st == NST_ALL - 1 and tsub == 3)
                    ins = None
                    for gi in range(8):
                        pst, off = vk_ps[gi]
                        ins = t.matmul(
                            pst[:, off:off + 132],
                            ktm[par][:, gi * 128:(gi + 1) * 128],
                            vpad[par][:, gi * 132:(gi + 1) * 132],
                            start=(first and gi in (0, 3, 6)),
                            stop=(last and gi in (2, 5, 7)),
                        )
                    ins.then_inc(s_vk, 1)
            t.wait_ge(s_wt, 16 * 16)
            for qst in range(NST_EXT):
                T = qst_T(qst)
                t.wait_ge(s_xt, NST_ALL * 128 + (qst + 1) * 128)
                for oc in range(8):
                    qg = qst * 8 + oc
                    if qg >= 2:
                        t.wait_ge(s_qe, qg - 1)
                    ins = None
                    for kc in range(8):
                        ins = t.matmul(
                            psQ[oc % 2][:, 0:T],
                            wq_sb[:, kc * 1024 + oc * 128:
                                  kc * 1024 + (oc + 1) * 128],
                            xfm[qst % 2][:, kc * 512:kc * 512 + T],
                            start=(kc == 0), stop=(kc == 7),
                        )
                    ins.then_inc(s_qm, 1)

        @blk.scalar
        def _(s):
            for st in range(NST_ALL):
                for tsub in range(4):
                    par = tsub % 2
                    vkt = st * 4 + tsub
                    for og in range(4):
                        kvg = vkt * 4 + og
                        s.wait_ge(s_kv, kvg + 1)
                        if vkt >= 2 and og == 0:
                            s.wait_ge(s_vk, vkt - 1)
                        if og < 2:
                            s.activation(
                                ktm[par][:, og * 512:(og + 1) * 512],
                                psKV[og % 2][:, :], AF.Relu,
                            ).then_inc(s_ev, 1)
                        else:
                            hh0 = 16 * (og - 2)
                            s.activation(
                                vpad[par][:, :]
                                .rearrange("p (h e) -> p h e", e=33)
                                [:, hh0:hh0 + 16, 0:32],
                                psKV[og % 2][:, :]
                                .rearrange("p (h d) -> p h d", d=32),
                                AF.Relu,
                            ).then_inc(s_ev, 1)
            for qst in range(NST_EXT):
                T = qst_T(qst)
                for oc in range(8):
                    qg = qst * 8 + oc
                    s.wait_ge(s_qm, qg + 1)
                    s.activation(
                        qfm[:, oc * SEXT + qst * 512:
                            oc * SEXT + qst * 512 + T],
                        psQ[oc % 2][:, 0:T], AF.Relu,
                    ).then_inc(s_qe, 1)

        @blk.vector
        def _(v):
            for par in range(2):
                v.memset(
                    vpad[par][:, :].rearrange("p (h e) -> p h e", e=33)
                    [:, :, 32:33], 1.0)
            v.memset(vkbd[:, :], 0.0)
            v.wait_ge(s_vk, NST_ALL * 4)
            ins = None
            for gi in range(8):
                pst, off = vk_ps[gi]
                for j in range(4):
                    ins = v.tensor_copy(
                        vkbd[32 * j:32 * (j + 1),
                             gi * 132 + 33 * j:gi * 132 + 33 * (j + 1)],
                        pst[32 * j:32 * (j + 1),
                            off + 33 * j:off + 33 * (j + 1)],
                    )
            ins.then_inc(s_vk, 1)

    freesems()
    if BUILD_UPTO == 1:
        qfm_g.__exit__(None, None, None)
        for cx in reversed(wctx):
            cx.__exit__(None, None, None)
        return nc

    snapsems()
    with (
        n.sbuf_tensor("pden", [128, 64], F32) as pden,
        n.sbuf_tensor("ptm0", [128, 1024], BF16) as ptm0,
        n.sbuf_tensor("ptm1", [128, 1024], BF16) as ptm1,
        n.psum_tensor("psVQ0a", [128, 512], F32) as psVQ0a,
        n.psum_tensor("psVQ0b", [128, 512], F32) as psVQ0b,
        n.psum_tensor("psVQ0c", [128, 512], F32) as psVQ0c,
        n.psum_tensor("psVQ1a", [128, 512], F32) as psVQ1a,
        n.psum_tensor("psVQ1b", [128, 512], F32) as psVQ1b,
        n.psum_tensor("psVQ1c", [128, 512], F32) as psVQ1c,
        n.Block("p1bA") as blk,
    ):
        ptm = [ptm0, ptm1]
        psVQ = [[psVQ0a, psVQ0b, psVQ0c], [psVQ1a, psVQ1b, psVQ1c]]
        gmap = [(0, 0), (0, 132), (0, 264), (1, 0), (1, 132), (1, 264),
                (2, 0), (2, 132)]

        s_vm = n.alloc_semaphore("s_vm")
        s_fza = n.alloc_semaphore("s_fza")
        s_dv = n.alloc_semaphore("s_dv")
        s_pw = n.alloc_semaphore("s_pw")

        @blk.tensor
        def _(t):
            for i in range(NT):
                par = i % 2
                if i >= 2:
                    t.wait_ge(s_dv, i - 1)
                ins = None
                for gi in range(8):
                    ti, off = gmap[gi]
                    ins = t.matmul(
                        psVQ[par][ti][:, off:off + 132],
                        qfm[:, gi * SEXT + i * 128:gi * SEXT + (i + 1) * 128],
                        vkbd[:, gi * 132:(gi + 1) * 132],
                        start=(gi in (0, 3, 6)), stop=(gi in (2, 5, 7)),
                    )
                ins.then_inc(s_vm, 1)

        @blk.vector
        def _(v):
            fz = _Fence(v, s_fza)
            for i in range(NT):
                par = i % 2
                v.wait_ge(s_vm, i + 1)
                if i >= 2:
                    v.wait_ge(s_pw, 16 * (i - 1))
                for ti, ng in ((0, 3), (1, 3), (2, 2)):
                    fz(v.tensor_single_scalar(
                        pden[:, ti * 12:ti * 12 + ng * 4]
                        .rearrange("p (g h) -> p g h", h=4),
                        psVQ[par][ti][:, 0:ng * 132]
                        .rearrange("p (g h e) -> p (g h) e", h=4, e=33)
                        [:, :, 32:33].rearrange("p (g h) e -> p g (h e)", h=4),
                        EPS0, ALU.add))
                fz(v.reciprocal(pden[:, 32:64], pden[:, 0:32]))
                ins = None
                for ti, ng in ((0, 3), (1, 3), (2, 2)):
                    g0 = ti * 3
                    ins = v.tensor_tensor(
                        ptm[par][:, g0 * 128:(g0 + ng) * 128]
                        .rearrange("p (g h e) -> p g h e", h=4, e=32),
                        psVQ[par][ti][:, 0:ng * 132]
                        .rearrange("p (g h e) -> p g h e", h=4, e=33)
                        [:, :, :, 0:32],
                        pden[:, 32 + ti * 12:32 + ti * 12 + ng * 4]
                        .rearrange("p (g h) -> p g h", h=4)
                        [:, :, :, None].broadcast_to((128, ng, 4, 32)),
                        ALU.mult)
                ins.then_inc(s_dv, 1)

        @blk.sync
        def _(sy):
            for i in range(NT):
                sy.wait_ge(s_dv, i + 1)
                sy.dma_start(out=pin_bf[i * 128:(i + 1) * 128, :],
                             in_=ptm[i % 2][:, :]).then_inc(s_pw, 16)
            sy.wait_ge(s_pw, 16 * NT)

    freesems()
    qfm_g.__exit__(None, None, None)

    snapsems()
    with (
        n.sbuf_tensor("wtm0", [128, 1024], F32) as wtm0,
        n.sbuf_tensor("wtm1", [128, 1024], F32) as wtm1,
        n.sbuf_tensor("wdt0", [128, 1024], BF16) as wdt0,
        n.sbuf_tensor("wdt1", [128, 1024], BF16) as wdt1,
        n.sbuf_tensor("wqs", [128, 32], F32) as wqs,
        n.sbuf_tensor("wqs2", [128, 32], F32) as wqs2,
        n.sbuf_tensor("wqrec", [128, 32], F32) as wqrec,
        n.sbuf_tensor("wqu", [128, 1024], F32) as wqu,
        n.sbuf_tensor("wqau", [128, 1024], F32) as wqau,
        n.sbuf_tensor("wqsg", [128, 1024], F32) as wqsg,
        n.sbuf_tensor("wqv8", [128, 1024], FP8) as wqv8,
        n.Block("p1w") as blk,
    ):
        wtm = [wtm0, wtm1]
        wdt = [wdt0, wdt1]
        s_wl = n.alloc_semaphore("s_wl")
        s_fzw = n.alloc_semaphore("s_fzw")
        s_qa = n.alloc_semaphore("s_qa")
        s_qb = n.alloc_semaphore("s_qb")
        s_qc = n.alloc_semaphore("s_qc")
        s_ws = n.alloc_semaphore("s_ws")
        s_wq2 = n.alloc_semaphore("s_wq2")
        s_br = n.alloc_semaphore("s_br")

        @blk.sync
        def _(sy):
            for j in range(8):
                if j >= 2:
                    sy.wait_ge(s_qa, j - 1)
                sy.dma_start(out=wtm[j % 2][:, :],
                             in_=w_ffn[j * 128:(j + 1) * 128, :]
                             ).then_inc(s_wl, 16)
            sy.wait_ge(s_ws, 16 * 8)
            for kc in range(8):
                sy.dma_start_transpose(
                    wd_sb[:, kc * 1024:(kc + 1) * 1024],
                    wd_bf[0:1024, kc * 128:(kc + 1) * 128],
                ).then_inc(s_wq2, 16)
            for kc in range(8):
                sy.dma_start_transpose(
                    wp_sb[:, kc * 1024:(kc + 1) * 1024],
                    wproj_bf[0:1024, kc * 128:(kc + 1) * 128],
                ).then_inc(s_wq2, 16)
            sy.wait_ge(s_wq2, 16 * 16)
            sy.wait_ge(s_br, 32)
            sy.wait_ge(s_wl, 16 * 8)

        @blk.gpsimd
        def _(g):
            g.dma_start(out=bp_row[:, :], in_=b_proj).then_inc(s_br, 16)
            g.dma_start(out=bfn_row[:, :], in_=b_ffn).then_inc(s_br, 16)
            for j in range(8):
                g.wait_ge(s_qc, j + 1)
                g.dma_start(out=wd_bf[j * 128:(j + 1) * 128, :],
                            in_=wdt[j % 2][:, :]).then_inc(s_ws, 16)

        @blk.vector
        def _(v):
            fz = _Fence(v, s_fzw)
            for j in range(8):
                par = j % 2
                v.wait_ge(s_wl, 16 * (j + 1))
                _quant_dve_pre(v, wtm[par][:, :], wqs, wqs2, wqrec, wqu,
                               fz).then_inc(s_qa, 1)
                v.wait_ge(s_qb, j + 1)
                if j >= 2:
                    v.wait_ge(s_ws, 16 * (j - 1))
                _quant_dve_post(v, wqau, wqsg, wqu, wqv8, wqs2,
                                wdt[par][:, :]).then_inc(s_qc, 1)

        @blk.scalar
        def _(s):
            for j in range(8):
                s.wait_ge(s_qa, j + 1)
                s.activation(wqau[:, :], wqu[:, :], AF.Abs)
                s.activation(wqsg[:, :], wqu[:, :], AF.Sign).then_inc(s_qb, 1)



    freesems()
    NXW = 9
    from contextlib import ExitStack
    snapsems()
    with ExitStack() as _es:
        def _sb(name, shape, dt):
            return _es.enter_context(n.sbuf_tensor(name, shape, dt))

        def _ps(name):
            return _es.enter_context(n.psum_tensor(name, [128, 512], F32))

        xtm0 = _sb("xtm0", [128, 1024], F32)
        xtm1 = _sb("xtm1", [128, 1024], F32)
        pfm0 = _sb("pfm0", [128, 8 * 512], BF16)
        pfm1 = _sb("pfm1", [128, 8 * 512], BF16)
        dfm0 = _sb("dfm0", [128, 8 * 512], BF16)
        dfm1 = _sb("dfm1", [128, 8 * 512], BF16)
        xw_all = _sb("xw", [128, NXW * 1024], F32)
        xw20 = _sb("xw20", [128, 1024], F32)
        xw21 = _sb("xw21", [128, 1024], F32)
        xdt0 = _sb("xdt0", [128, 1024], BF16)
        xdt1 = _sb("xdt1", [128, 1024], BF16)
        lncol = _sb("lncol", [128, 16], F32)
        qqs = [_sb("qqsA", [128, 32], F32), _sb("qqsB", [128, 32], F32)]
        qqs2 = [_sb("qqs2A", [128, 32], F32), _sb("qqs2B", [128, 32], F32)]
        qqrec = [_sb("qqrecA", [128, 32], F32),
                 _sb("qqrecB", [128, 32], F32)]
        qqu = [_sb("qquA", [128, 1024], F32), _sb("qquB", [128, 1024], F32)]
        qqau = [_sb("qqauA", [128, 1024], F32),
                _sb("qqauB", [128, 1024], F32)]
        qqsg = [_sb("qqsgA", [128, 1024], F32),
                _sb("qqsgB", [128, 1024], F32)]
        qqv8 = [_sb("qqv8A", [128, 1024], FP8),
                _sb("qqv8B", [128, 1024], FP8)]
        lnscr = _sb("lnscr", [128, 1024], F32)
        psP0 = _ps("psP0")
        psP1 = _ps("psP1")
        psF0 = _ps("psF0")
        psF1 = _ps("psF1")
        blk = _es.enter_context(n.Block("p1bB"))
        xtm = [xtm0, xtm1]
        pfm = [pfm0, pfm1]
        dfm = [dfm0, dfm1]
        xw = [xw_all[:, k * 1024:(k + 1) * 1024] for k in range(NXW)]
        xw2 = [xw20, xw21]
        xdt = [xdt0, xdt1]
        psP = [psP0, psP1]
        psF = [psF0, psF1]

        s_xl = n.alloc_semaphore("s_xl")
        s_pt = n.alloc_semaphore("s_pt")
        s_pm = n.alloc_semaphore("s_pm")
        s_c = n.alloc_semaphore("s_c")
        s_l1 = n.alloc_semaphore("s_l1")
        s_sq1 = n.alloc_semaphore("s_sq1")
        s_fzb = n.alloc_semaphore("s_fzb")
        s_sq2 = n.alloc_semaphore("s_sq2")
        s_l1r = n.alloc_semaphore("s_l1r")
        s_qa = n.alloc_semaphore("s_qa2")
        s_qb = n.alloc_semaphore("s_qb2")
        s_q = n.alloc_semaphore("s_q")
        s_xd = n.alloc_semaphore("s_xd")
        s_dt = n.alloc_semaphore("s_dt")
        s_fm = n.alloc_semaphore("s_fm")
        s_f = n.alloc_semaphore("s_f")
        s_l2 = n.alloc_semaphore("s_l2")
        s_l2r = n.alloc_semaphore("s_l2r")
        s_fin = n.alloc_semaphore("s_fin")
        s_x2 = n.alloc_semaphore("s_x2")
        s_cst2 = n.alloc_semaphore("s_cst2")
        s_ca = n.alloc_semaphore("s_ca")
        s_fb = n.alloc_semaphore("s_fb")

        def sup_of(i):
            return min(i // 4, 8)

        @blk.sync
        def _(sy):
            supend = {t0 + nt - 1: u for u, (t0, nt) in enumerate(SUPS)}
            supstart = {t0: u for u, (t0, nt) in enumerate(SUPS)}
            for i in range(NT + 8):
                j = i - 3
                if 0 <= j < NT:
                    sy.wait_ge(s_q, j + 1)
                    sy.dma_start(out=xd_bf[j * 128:(j + 1) * 128, :],
                                 in_=xdt[j % 2][:, :]).then_inc(s_xd, 16)
                if 0 <= j < NT and j in supend:
                    u = supend[j]
                    t0, nt = SUPS[u]
                    T = nt * 128
                    if u >= 2:
                        pt0, pnt = SUPS[u - 2]
                        sy.wait_ge(s_fm, 2 * (pt0 + pnt))
                    sy.wait_ge(s_xd, 16 * (t0 + nt))
                    for kc in range(8):
                        sy.dma_start_transpose(
                            dfm[u % 2][:, kc * 512:kc * 512 + T],
                            xd_bf[t0 * 128:t0 * 128 + T,
                                  kc * 128:(kc + 1) * 128],
                        ).then_inc(s_dt, 16)
                k = i - 8
                if 0 <= k < NT:
                    sy.wait_ge(s_fin, k + 1)
                    sy.dma_start(out=x2_tm[k * 128:(k + 1) * 128, :],
                                 in_=xw2[k % 2][:, :]).then_inc(s_x2, 16)
                if i < NT:
                    if i in supstart:
                        u = supstart[i]
                        t0, nt = SUPS[u]
                        T = nt * 128
                        if u >= 2:
                            pt0, pnt = SUPS[u - 2]
                            sy.wait_ge(s_pm, 2 * (pt0 + pnt))
                        for kc in range(8):
                            sy.dma_start_transpose(
                                pfm[u % 2][:, kc * 512:kc * 512 + T],
                                pin_bf[t0 * 128:t0 * 128 + T,
                                       kc * 128:(kc + 1) * 128],
                            ).then_inc(s_pt, 16)
                    if i >= 2:
                        sy.wait_ge(s_c, i - 1)
                    sy.dma_start(out=xtm[i % 2][:, :],
                                 in_=x_ext[i * 128:(i + 1) * 128, :]
                                 ).then_inc(s_xl, 16)
            sy.wait_ge(s_x2, 16 * NT)
            sy.wait_ge(s_xl, 16 * NT)
            sy.wait_ge(s_xd, 16 * NT)
            sy.wait_ge(s_pt, 16 * 8 * len(SUPS))
            sy.wait_ge(s_dt, 16 * 8 * len(SUPS))

        @blk.gpsimd
        def _(g):
            g.wait_ge(s_x2, 16 * NT)
            g.dma_start(out=x2_bf, in_=x2_tm).then_inc(s_cst2, 16)
            g.wait_ge(s_cst2, 16)

        @blk.tensor
        def _(t):
            def proj_mms(u):
                t0, nt = SUPS[u]
                t.wait_ge(s_pt, (u + 1) * 128)
                for k in range(nt):
                    i = t0 + k
                    if i >= 1:
                        t.wait_ge(s_c, i)
                    for oc in range(2):
                        ins = None
                        for kc in range(8):
                            ins = t.matmul(
                                psP[oc][:, :],
                                pfm[u % 2][:, kc * 512 + k * 128:
                                           kc * 512 + k * 128 + 128],
                                wp_sb[:, kc * 1024 + oc * 512:
                                      kc * 1024 + (oc + 1) * 512],
                                start=(kc == 0), stop=False,
                            )
                        ins = t.matmul(
                            psP[oc][:, :], onesr[:, 0:128],
                            bp_row[:, oc * 512:(oc + 1) * 512],
                            start=False, stop=True)
                        ins.then_inc(s_pm, 1)

            def ffn_mms(u):
                t0, nt = SUPS[u]
                t.wait_ge(s_dt, (u + 1) * 128)
                for k in range(nt):
                    i = t0 + k
                    if i >= 1:
                        t.wait_ge(s_f, i)
                    for oc in range(2):
                        ins = None
                        for kc in range(8):
                            ins = t.matmul(
                                psF[oc][:, :],
                                dfm[u % 2][:, kc * 512 + k * 128:
                                           kc * 512 + k * 128 + 128],
                                wd_sb[:, kc * 1024 + oc * 512:
                                      kc * 1024 + (oc + 1) * 512],
                                start=(kc == 0), stop=False,
                            )
                        ins = t.matmul(
                            psF[oc][:, :], onesr[:, 0:128],
                            bfn_row[:, oc * 512:(oc + 1) * 512],
                            start=False, stop=True)
                        ins.then_inc(s_fm, 1)

            proj_mms(0)
            for u in range(1, len(SUPS)):
                proj_mms(u)
                ffn_mms(u - 1)
            ffn_mms(len(SUPS) - 1)

        @blk.vector
        def _(v):
            fz = _Fence(v, s_fzb)

            def evF_A(k):
                w = xw[k % NXW]
                b = 8 + 4 * (k % 2)
                v.wait_ge(s_fm, 2 * k + 2)
                if k >= 2:
                    v.wait_ge(s_x2, 16 * (k - 1))
                v.tensor_add(xw2[k % 2][:, 0:512], psF[0][:, :], w[:, 0:512])
                v.tensor_add(xw2[k % 2][:, 512:1024], psF[1][:, :],
                             w[:, 512:1024]).then_inc(s_f, 1)
                v.tensor_scalar(xw2[k % 2][:, :], xw2[k % 2][:, :], 1.0,
                                0.0, ALU.mult, ALU.add,
                                accum_out=lncol[:, b:b + 1]).then_inc(s_fb, 1)

            def evF_B(k):
                b = 8 + 4 * (k % 2)
                v.wait_ge(s_fb, k + 1)
                v.tensor_single_scalar(lncol[:, b:b + 1], lncol[:, b:b + 1],
                                       INV_C, ALU.mult)
                v.wait_ge(s_sq2, k + 1)
                v.tensor_single_scalar(lncol[:, b + 1:b + 2],
                                       lncol[:, b + 1:b + 2], INV_C,
                                       ALU.mult)
                fz(v.tensor_tensor(lncol[:, b + 2:b + 3], lncol[:, b:b + 1],
                                   lncol[:, b:b + 1], ALU.mult))
                fz(v.tensor_tensor(lncol[:, b + 1:b + 2],
                                   lncol[:, b + 1:b + 2],
                                   lncol[:, b + 2:b + 3], ALU.subtract))
                v.tensor_single_scalar(lncol[:, b + 1:b + 2],
                                       lncol[:, b + 1:b + 2], LN_EPS,
                                       ALU.add).then_inc(s_l2, 1)
                v.wait_ge(s_l2r, k + 1)
                v.tensor_scalar(xw2[k % 2][:, :], xw2[k % 2][:, :],
                                lncol[:, b:b + 1], lncol[:, b + 3:b + 4],
                                ALU.subtract, ALU.mult).then_inc(s_fin, 1)

            def evC_A(i):
                w = xw[i % NXW]
                b = 4 * (i % 2)
                v.wait_ge(s_pm, 2 * i + 2)
                v.wait_ge(s_xl, 16 * (i + 1))
                if i >= NXW:
                    v.wait_ge(s_f, i - NXW + 1)
                v.tensor_add(w[:, 0:512], psP[0][:, :], xtm[i % 2][:, 0:512])
                v.tensor_add(w[:, 512:1024], psP[1][:, :],
                             xtm[i % 2][:, 512:1024]).then_inc(s_c, 1)
                v.tensor_scalar(w[:, :], w[:, :], 1.0, 0.0, ALU.mult,
                                ALU.add,
                                accum_out=lncol[:, b:b + 1]).then_inc(s_ca, 1)

            def evC_B(i):
                w = xw[i % NXW]
                b = 4 * (i % 2)
                qp = i % 2
                v.wait_ge(s_ca, i + 1)
                v.tensor_single_scalar(lncol[:, b:b + 1], lncol[:, b:b + 1],
                                       INV_C, ALU.mult)
                v.wait_ge(s_sq1, i + 1)
                v.tensor_single_scalar(lncol[:, b + 1:b + 2],
                                       lncol[:, b + 1:b + 2], INV_C,
                                       ALU.mult)
                fz(v.tensor_tensor(lncol[:, b + 2:b + 3], lncol[:, b:b + 1],
                                   lncol[:, b:b + 1], ALU.mult))
                fz(v.tensor_tensor(lncol[:, b + 1:b + 2],
                                   lncol[:, b + 1:b + 2],
                                   lncol[:, b + 2:b + 3], ALU.subtract))
                v.tensor_single_scalar(lncol[:, b + 1:b + 2],
                                       lncol[:, b + 1:b + 2], LN_EPS,
                                       ALU.add).then_inc(s_l1, 1)
                v.wait_ge(s_l1r, i + 1)
                v.tensor_scalar(w[:, :], w[:, :], lncol[:, b:b + 1],
                                lncol[:, b + 3:b + 4], ALU.subtract,
                                ALU.mult)
                _quant_dve_pre(v, w[:, :], qqs[qp], qqs2[qp], qqrec[qp],
                               qqu[qp], fz).then_inc(s_qa, 1)

            def evC_C(i):
                qp = i % 2
                v.wait_ge(s_qb, i + 1)
                if i >= 2:
                    v.wait_ge(s_xd, 16 * (i - 1))
                _quant_dve_post(v, qqau[qp], qqsg[qp], qqu[qp], qqv8[qp],
                                qqs2[qp], xdt[i % 2][:, :]).then_inc(s_q, 1)

            def pipe2(tiles, A, B):
                for idx, x in enumerate(tiles):
                    A(x)
                    if idx >= 1:
                        B(tiles[idx - 1])
                B(tiles[-1])

            def pipe3(tiles, A, B, C):
                for idx, x in enumerate(tiles):
                    A(x)
                    if idx >= 1:
                        B(tiles[idx - 1])
                    if idx >= 2:
                        C(tiles[idx - 2])
                B(tiles[-1])
                if len(tiles) >= 2:
                    C(tiles[-2])
                C(tiles[-1])

            NU = len(SUPS)
            for u in range(NU + 2):
                if 2 <= u < NU + 2:
                    t0, nt = SUPS[u - 2]
                    pipe2(list(range(t0, t0 + nt)), evF_A, evF_B)
                if u < NU:
                    t0, nt = SUPS[u]
                    pipe3(list(range(t0, t0 + nt)), evC_A, evC_B, evC_C)

        @blk.scalar
        def _(s):
            def sc_sq2(k):
                b = 8 + 4 * (k % 2)
                s.wait_ge(s_f, k + 1)
                s.activation(lnscr[:, :], xw2[k % 2][:, :], AF.Square,
                             accum_out=lncol[:, b + 1:b + 2]
                             ).then_inc(s_sq2, 1)

            def sc_l2(k):
                b = 8 + 4 * (k % 2)
                s.wait_ge(s_l2, k + 1)
                s.activation(lncol[:, b + 3:b + 4], lncol[:, b + 1:b + 2],
                             AF.Ln)
                s.activation(lncol[:, b + 3:b + 4], lncol[:, b + 3:b + 4],
                             AF.Exp, scale=-0.5).then_inc(s_l2r, 1)

            def sc_sq1(i):
                b = 4 * (i % 2)
                s.wait_ge(s_c, i + 1)
                s.activation(lnscr[:, :], xw[i % NXW][:, :], AF.Square,
                             accum_out=lncol[:, b + 1:b + 2]
                             ).then_inc(s_sq1, 1)

            def sc_l1(i):
                b = 4 * (i % 2)
                s.wait_ge(s_l1, i + 1)
                s.activation(lncol[:, b + 3:b + 4], lncol[:, b + 1:b + 2],
                             AF.Ln)
                s.activation(lncol[:, b + 3:b + 4], lncol[:, b + 3:b + 4],
                             AF.Exp, scale=-0.5).then_inc(s_l1r, 1)

            def sc_ab(i):
                qp = i % 2
                s.wait_ge(s_qa, i + 1)
                s.activation(qqau[qp][:, :], qqu[qp][:, :], AF.Abs)
                s.activation(qqsg[qp][:, :], qqu[qp][:, :],
                             AF.Sign).then_inc(s_qb, 1)

            def pipe2(tiles, A, B):
                for idx, x in enumerate(tiles):
                    A(x)
                    if idx >= 1:
                        B(tiles[idx - 1])
                B(tiles[-1])

            def pipe3(tiles, A, B, C):
                for idx, x in enumerate(tiles):
                    A(x)
                    if idx >= 1:
                        B(tiles[idx - 1])
                    if idx >= 2:
                        C(tiles[idx - 2])
                B(tiles[-1])
                if len(tiles) >= 2:
                    C(tiles[-2])
                C(tiles[-1])

            NU = len(SUPS)
            for u in range(NU + 2):
                if 2 <= u < NU + 2:
                    t0, nt = SUPS[u - 2]
                    pipe2(list(range(t0, t0 + nt)), sc_sq2, sc_l2)
                if u < NU:
                    t0, nt = SUPS[u]
                    pipe3(list(range(t0, t0 + nt)), sc_sq1, sc_l1, sc_ab)

    for cx in reversed(wctx):
        cx.__exit__(None, None, None)
    freesems()

    if BUILD_UPTO == 2:
        return nc

    def vgi(st, k, og):
        return st * 8 + k * 2 + og

    def qst_T2(st):
        return 512 if st < 8 else SEXT - 8 * 512

    snapsems()
    with ExitStack() as _es:
        def _sb(name, shape, dt):
            return _es.enter_context(n.sbuf_tensor(name, shape, dt))

        xfm20 = _sb("xfm20", [128, 8 * 512], BF16)
        xfm21 = _sb("xfm21", [128, 8 * 512], BF16)
        wq1 = _sb("wq1", [128, 8 * 1024], BF16)
        wk1 = _sb("wk1", [128, 8 * 1024], BF16)
        wv1 = _sb("wv1", [128, 8 * 1024], BF16)
        bqc_sb = _sb("bqc_sb", [128, 8], F32)
        bkc_sb = _sb("bkc_sb", [128, 8], F32)
        bqs_sb = _sb("bqs_sb", [128, 8], F32)
        bv_row = _sb("bv_row", [1, 1024], F32R)
        qt_sb = _sb("qt_sb", [128, 8 * 512], BF16)
        kt_sb = _sb("kt_sb", [128, 8 * 512], BF16)
        vt0 = _sb("vt0", [128, 1040], BF16)
        vt1 = _sb("vt1", [128, 1040], BF16)
        psQ0 = _es.enter_context(n.psum_tensor("ps2q0", [128, 512], F32))
        psQ1 = _es.enter_context(n.psum_tensor("ps2q1", [128, 512], F32))
        psK0 = _es.enter_context(n.psum_tensor("ps2k0", [128, 512], F32))
        psK1 = _es.enter_context(n.psum_tensor("ps2k1", [128, 512], F32))
        psV0 = _es.enter_context(n.psum_tensor("ps2v0", [128, 512], F32))
        psV1 = _es.enter_context(n.psum_tensor("ps2v1", [128, 512], F32))
        blk = _es.enter_context(n.Block("p2qkv"))

        xfm2 = [xfm20, xfm21]
        qkt = [qt_sb, kt_sb]
        vt = [vt0, vt1]
        psQK = [[psQ0, psQ1], [psK0, psK1]]
        psV = [psV0, psV1]

        c_wl = n.alloc_semaphore("c_wl")
        c_bq = n.alloc_semaphore("c_bq")
        c_xt = n.alloc_semaphore("c_xt")
        c_qkm = n.alloc_semaphore("c_qkm")
        c_qke = n.alloc_semaphore("c_qke")
        c_qks = n.alloc_semaphore("c_qks")
        c_vm = n.alloc_semaphore("c_vm")
        c_ve = n.alloc_semaphore("c_ve")
        c_vs = n.alloc_semaphore("c_vs")

        @blk.gpsimd
        def _(g):
            g.dma_start(out=wq1[:, :].rearrange("p (kc o) -> p kc o", kc=8),
                        in_=wq_i.rearrange("(kc p) o -> p kc o", p=128)
                        ).then_inc(c_wl, 16)
            g.dma_start(out=wk1[:, :].rearrange("p (kc o) -> p kc o", kc=8),
                        in_=wk_i.rearrange("(kc p) o -> p kc o", p=128)
                        ).then_inc(c_wl, 16)
            g.dma_start(out=wv1[:, :].rearrange("p (kc o) -> p kc o", kc=8),
                        in_=wv_i.rearrange("(kc p) o -> p kc o", p=128)
                        ).then_inc(c_wl, 16)
            g.dma_start(out=bv_row[:, :], in_=bv_i).then_inc(c_wl, 16)
            g.dma_start(out=bqc_sb[:, :], in_=bqc_i).then_inc(c_wl, 16)
            g.dma_start(out=bkc_sb[:, :], in_=bkc_i).then_inc(c_wl, 16)
            for st in range(NST_EXT):
                nt = (qst_T2(st) + 127) // 128
                for k in range(nt):
                    i = st * 4 + k
                    g.wait_ge(c_ve, vgi(st, k, 1) + 1)
                    g.dma_start(out=vpad_d[i * 128:(i + 1) * 128, :],
                                in_=vt[i % 2][:, :]).then_inc(c_vs, 16)
            g.wait_ge(c_vs, 16 * NT)

        @blk.sync
        def _(sy):
            for st in range(NST_EXT):
                T = qst_T2(st)
                if st >= 2:
                    sy.wait_ge(c_vm, vgi(st - 2, (qst_T2(st - 2) + 127)
                                         // 128 - 1, 1) + 1)
                for kc in range(8):
                    sy.dma_start_transpose(
                        xfm2[st % 2][:, kc * 512:kc * 512 + T],
                        x2_bf[st * 512:st * 512 + T,
                              kc * 128:(kc + 1) * 128],
                    ).then_inc(c_xt, 16)
                for w in range(2):
                    sy.wait_ge(c_qke, (st * 2 + w) * 8 + 8)
                    dst = qfm_d if w == 0 else kfm_d
                    sy.dma_start(
                        out=dst.rearrange("(oc p) t -> p oc t", p=128)
                        [:, :, st * 512:st * 512 + T],
                        in_=qkt[w][:, :].rearrange("p (oc t) -> p oc t", oc=8)
                        [:, :, 0:T],
                    ).then_inc(c_qks, 16)
            sy.wait_ge(c_qks, 16 * NST_EXT * 2)
            sy.wait_ge(c_xt, 16 * 8 * NST_EXT)
            sy.wait_ge(c_wl, 16 * 6)

        @blk.tensor
        def _(t):
            t.wait_ge(c_wl, 16 * 6)
            for st in range(NST_EXT):
                T = qst_T2(st)
                nt = (T + 127) // 128
                t.wait_ge(c_xt, 16 * 8 * (st + 1))
                for w in range(2):
                    wsb = wq1 if w == 0 else wk1
                    for oc in range(8):
                        gidx = (st * 2 + w) * 8 + oc
                        if oc >= 2:
                            t.wait_ge(c_qke, gidx - 1)
                        elif st >= 1:
                            t.wait_ge(c_qks, 16 * ((st - 1) * 2 + w + 1))
                        ins = None
                        for kc in range(8):
                            ins = t.matmul(
                                psQK[w][oc % 2][:, 0:T],
                                wsb[:, kc * 1024 + oc * 128:
                                    kc * 1024 + (oc + 1) * 128],
                                xfm2[st % 2][:, kc * 512:kc * 512 + T],
                                start=(kc == 0), stop=(kc == 7),
                            )
                        ins.then_inc(c_qkm, 1)
                for k in range(nt):
                    for og in range(2):
                        vg = vgi(st, k, og)
                        if vg >= 2:
                            t.wait_ge(c_ve, vg - 1)
                        ins = None
                        for kc in range(8):
                            ins = t.matmul(
                                psV[og % 2][:, :],
                                xfm2[st % 2][:, kc * 512 + k * 128:
                                             kc * 512 + k * 128 + 128],
                                wv1[:, kc * 1024 + og * 512:
                                    kc * 1024 + (og + 1) * 512],
                                start=(kc == 0), stop=False,
                            )
                        ins = t.matmul(
                            psV[og % 2][:, :], onesr[:, 0:128],
                            bv_row[:, og * 512:(og + 1) * 512],
                            start=False, stop=True)
                        ins.then_inc(c_vm, 1)

        @blk.scalar
        def _(s):
            for st in range(NST_EXT):
                T = qst_T2(st)
                nt = (T + 127) // 128
                for w in range(2):
                    for oc in range(8):
                        gidx = (st * 2 + w) * 8 + oc
                        s.wait_ge(c_qkm, gidx + 1)
                        if st >= 1 and oc == 0:
                            s.wait_ge(c_qks, 16 * ((st - 1) * 2 + w + 1))
                        if st == 0 and w == 0 and oc == 0:
                            s.wait_ge(c_bq, 1)
                        if w == 0:
                            s.activation(
                                qkt[w][:, oc * 512:oc * 512 + T],
                                psQK[w][oc % 2][:, 0:T], AF.Identity,
                                scale=0.125,
                                bias=bqs_sb[:, oc:oc + 1]).then_inc(c_qke, 1)
                        else:
                            s.activation(
                                qkt[w][:, oc * 512:oc * 512 + T],
                                psQK[w][oc % 2][:, 0:T], AF.Identity,
                                bias=bkc_sb[:, oc:oc + 1]).then_inc(c_qke, 1)
                for k in range(nt):
                    i = st * 4 + k
                    for og in range(2):
                        vg = vgi(st, k, og)
                        s.wait_ge(c_vm, vg + 1)
                        if og == 0 and i >= 2:
                            s.wait_ge(c_vs, 16 * (i - 1))
                        s.activation(
                            vt[i % 2][:, :]
                            .rearrange("p (h e) -> p h e", e=65)
                            [:, og * 8:(og + 1) * 8, 0:64],
                            psV[og % 2][:, :]
                            .rearrange("p (h d) -> p h d", d=64),
                            AF.Copy,
                        ).then_inc(c_ve, 1)

        @blk.vector
        def _(v):
            for par in range(2):
                v.memset(
                    vt[par][:, :].rearrange("p (h e) -> p h e", e=65)
                    [:, :, 64:65], 1.0)
            v.wait_ge(c_wl, 16 * 6)
            v.tensor_single_scalar(bqs_sb[:, :], bqc_sb[:, :], 0.125,
                                   ALU.mult).then_inc(c_bq, 1)

    freesems()
    if BUILD_UPTO == 3:
        return nc

    NQC = SOWN // 128
    snapsems()
    with ExitStack() as _es:
        def _sb(name, shape, dt):
            return _es.enter_context(n.sbuf_tensor(name, shape, dt))

        qwin0 = _sb("qwin0", [128, 8 * 512], BF16)
        qwin1 = _sb("qwin1", [128, 8 * 512], BF16)
        kwin0 = _sb("kwin0", [128, 8 * 768], BF16)
        kwin1 = _sb("kwin1", [128, 8 * 768], BF16)
        vwin0 = _sb("vwin0", [128, 6 * 1040], BF16)
        vwin1 = _sb("vwin1", [128, 6 * 1040], BF16)
        biasT_sb = _sb("biasT_sb", [128, H1 * 3 * 128], F32)
        ebias_sb = _sb("ebias_sb", [128, H1 * 3 * 128], BF16)
        ebL_sb = _sb("ebL_sb", [128, H1 * 128], BF16)
        ebR_sb = _sb("ebR_sb", [128, H1 * 128], BF16)
        ebsc = _sb("ebsc", [128, H1 * 128], F32)
        scb0 = _sb("scb0", [128, 384], F32)
        scb1 = _sb("scb1", [128, 384], F32)
        exq0 = _sb("exq0", [128, 384], BF16)
        exq1 = _sb("exq1", [128, 384], BF16)
        exb0 = _sb("exb0", [128, 384], BF16)
        exb1 = _sb("exb1", [128, 384], BF16)
        nrmc = _sb("nrmc", [128, 16], F32)
        att0 = _sb("att0", [128, 1024], BF16)
        att1 = _sb("att1", [128, 1024], BF16)
        psS0 = _es.enter_context(n.psum_tensor("psS0", [128, 512], F32))
        psS1 = _es.enter_context(n.psum_tensor("psS1", [128, 512], F32))
        psAV = _es.enter_context(n.psum_tensor("psAV", [128, 2048], F32))
        blk = _es.enter_context(n.Block("p2attn"))

        qwin = [qwin0, qwin1]
        kwin = [kwin0, kwin1]
        vwin = [vwin0, vwin1]
        scb = [scb0, scb1]
        exq = [exq0, exq1]
        exb = [exb0, exb1]
        att = [att0, att1]
        psS = [psS0, psS1]

        a_cn = n.alloc_semaphore("a_cn")
        a_fz = n.alloc_semaphore("a_fz")
        a_ebl = n.alloc_semaphore("a_ebl")
        a_cl = n.alloc_semaphore("a_cl")
        a_ebx = n.alloc_semaphore("a_ebx")
        a_qw = n.alloc_semaphore("a_qw")
        a_kw = n.alloc_semaphore("a_kw")
        a_vw = n.alloc_semaphore("a_vw")
        a_sc = n.alloc_semaphore("a_sc")
        a_ex = n.alloc_semaphore("a_ex")
        a_eb = n.alloc_semaphore("a_eb")
        a_av = n.alloc_semaphore("a_av")
        a_nm = n.alloc_semaphore("a_nm")
        a_st = n.alloc_semaphore("a_st")

        @blk.gpsimd
        def _(g):
            g.dma_start(out=biasT_sb[:, :]
                        .rearrange("p (h r q) -> p h r q", h=H1, r=3),
                        in_=biasT_i.rearrange("h r w q -> w h r q")
                        ).then_inc(a_cn, 16)
            g.dma_start(out=ebsc[:, :].rearrange("p (h q) -> p h q", h=H1),
                        in_=biasTL_i.rearrange("h w q -> w h q")
                        ).then_inc(a_ebl, 16)
            g.wait_ge(a_ebx, 2)
            g.dma_start(out=ebsc[:, :].rearrange("p (h q) -> p h q", h=H1),
                        in_=biasTR_i.rearrange("h w q -> w h q")
                        ).then_inc(a_ebl, 16)

        @blk.sync
        def _(sy):
            for step in range(NQC + 2):
                qc = step - 2
                if qc >= 0:
                    sy.wait_ge(a_nm, qc + 1)
                    sy.dma_start(out=attn_bf[qc * 128:(qc + 1) * 128, :],
                                 in_=att[qc % 2][:, :]).then_inc(a_st, 16)
                if step < NQC and step % 4 == 0:
                    qb = step // 4
                    if qb >= 2:
                        sy.wait_ge(a_nm, 4 * (qb - 1))
                    for oc in range(8):
                        sy.dma_start(
                            out=qwin[qb % 2][:, oc * 512:(oc + 1) * 512],
                            in_=qfm_d[oc * 128:(oc + 1) * 128,
                                      128 + 512 * qb:128 + 512 * qb + 512],
                        ).then_inc(a_qw, 16)
                    for oc in range(8):
                        sy.dma_start(
                            out=kwin[qb % 2][:, oc * 768:(oc + 1) * 768],
                            in_=kfm_d[oc * 128:(oc + 1) * 128,
                                      512 * qb:512 * qb + 768],
                        ).then_inc(a_kw, 16)
                    for kv in range(6):
                        sy.dma_start(
                            out=vwin[qb % 2][:, kv * 1040:(kv + 1) * 1040],
                            in_=vpad_d[512 * qb + kv * 128:
                                       512 * qb + (kv + 1) * 128, :],
                        ).then_inc(a_vw, 16)
            sy.wait_ge(a_st, 16 * NQC)
            sy.wait_ge(a_qw, 16 * 64)
            sy.wait_ge(a_kw, 16 * 64)
            sy.wait_ge(a_vw, 16 * 48)

        @blk.tensor
        def _(t):

            def sc_mms(qc, h):
                qb = qc // 4
                m = qc - 4 * qb
                sidx = qc * 16 + h
                if sidx >= 2:
                    t.wait_ge(a_cl, sidx - 1)
                pb = psS[h % 2]
                hb = 64 * (h % 2)
                hc = h // 2
                ins = None
                for r in range(3):
                    ins = t.matmul(
                        pb[:, 128 * r:128 * (r + 1)],
                        kwin[qb % 2][hb:hb + 64,
                                     hc * 768 + (m + r) * 128:
                                     hc * 768 + (m + r + 1) * 128],
                        qwin[qb % 2][hb:hb + 64,
                                     hc * 512 + m * 128:
                                     hc * 512 + (m + 1) * 128],
                        start=(r == 0), stop=(r == 2))
                ins.then_inc(a_sc, 1)

            def av_mms(qc, h):
                qb = qc // 4
                m = qc - 4 * qb
                sidx = qc * 16 + h
                t.wait_ge(a_eb, sidx + 1)
                if h == 0 and qc >= 1:
                    t.wait_ge(a_nm, qc)
                ins = None
                for r in range(3):
                    ins = t.matmul(
                        psAV[:, 128 * h:128 * h + 65],
                        exb[h % 2][:, 128 * r:128 * (r + 1)],
                        vwin[qb % 2][:, (m + r) * 1040 + 65 * h:
                                     (m + r) * 1040 + 65 * (h + 1)],
                        start=(r == 0 and h % 4 == 0),
                        stop=(r == 2 and h % 4 == 3),
                    )
                ins.then_inc(a_av, 1)

            for qc in range(NQC):
                qb = qc // 4
                t.wait_ge(a_qw, 16 * 8 * (qb + 1))
                t.wait_ge(a_kw, 16 * 8 * (qb + 1))
                t.wait_ge(a_vw, 16 * 6 * (qb + 1))
                sc_mms(qc, 0)
                sc_mms(qc, 1)
                for h in range(16):
                    if h + 2 < 16:
                        sc_mms(qc, h + 2)
                    av_mms(qc, h)

        @blk.vector
        def _(v):
            fz = _Fence(v, a_fz)
            v.wait_ge(a_ebx, 3)

            def eb_mult(qc, h):
                sidx = qc * 16 + h
                v.wait_ge(a_ex, sidx + 1)
                if sidx >= 2:
                    v.wait_ge(a_av, sidx - 1)
                e = exb[h % 2]
                q = exq[h % 2]
                if qc == 0:
                    v.tensor_tensor(e[:, 0:128], q[:, 0:128],
                                    ebL_sb[:, h * 128:(h + 1) * 128],
                                    ALU.mult)
                    v.tensor_tensor(e[:, 128:384], q[:, 128:384],
                                    ebias_sb[:, h * 384 + 128:
                                             h * 384 + 384],
                                    ALU.mult).then_inc(a_eb, 1)
                elif qc == NQC - 1:
                    v.tensor_tensor(e[:, 0:256], q[:, 0:256],
                                    ebias_sb[:, h * 384:h * 384 + 256],
                                    ALU.mult)
                    v.tensor_tensor(e[:, 256:384], q[:, 256:384],
                                    ebR_sb[:, h * 128:(h + 1) * 128],
                                    ALU.mult).then_inc(a_eb, 1)
                else:
                    v.tensor_tensor(e[:, :], q[:, :],
                                    ebias_sb[:, h * 384:(h + 1) * 384],
                                    ALU.mult).then_inc(a_eb, 1)

            for qc in range(NQC):
                for h in range(16):
                    sidx = qc * 16 + h
                    v.wait_ge(a_sc, sidx + 1)
                    if sidx >= 2:
                        v.wait_ge(a_ex, sidx - 1)
                    v.tensor_single_scalar(
                        scb[h % 2][:, :], psS[h % 2][:, 0:384],
                        8.0 * CLAMP, ALU.min).then_inc(a_cl, 1)
                    if h >= 1:
                        eb_mult(qc, h - 1)
                eb_mult(qc, 15)
                v.wait_ge(a_av, 16 * (qc + 1))
                if qc >= 2:
                    v.wait_ge(a_st, 16 * (qc - 1))
                fz(v.reciprocal(
                    nrmc[:, :],
                    psAV[:, :].rearrange("p (h e) -> p h e", e=128)
                    [:, :, 64:65].rearrange("p h e -> p (h e)")))
                v.tensor_tensor(
                    att[qc % 2][:, :].rearrange("p (h d) -> p h d", d=64),
                    psAV[:, :].rearrange("p (h e) -> p h e", e=128)
                    [:, :, 0:64],
                    nrmc[:, :].rearrange("p (h o) -> p h o", o=1)
                    .broadcast_to((128, 16, 64)),
                    ALU.mult).then_inc(a_nm, 1)

        @blk.scalar
        def _(s):
            s.wait_ge(a_cn, 16)
            s.activation(ebias_sb[:, :], biasT_sb[:, :],
                         AF.Exp).then_inc(a_ebx, 1)
            s.wait_ge(a_ebl, 16)
            s.activation(ebL_sb[:, :], ebsc[:, :],
                         AF.Exp).then_inc(a_ebx, 1)
            s.wait_ge(a_ebl, 32)
            s.activation(ebR_sb[:, :], ebsc[:, :],
                         AF.Exp).then_inc(a_ebx, 1)
            for qc in range(NQC):
                for h in range(16):
                    sidx = qc * 16 + h
                    s.wait_ge(a_cl, sidx + 1)
                    if sidx >= 2:
                        s.wait_ge(a_eb, sidx - 1)
                    s.activation(exq[h % 2][:, :], scb[h % 2][:, :],
                                 AF.Exp, scale=0.125).then_inc(a_ex, 1)

    freesems()
    if BUILD_UPTO == 4:
        return nc

    NSP = SOWN // 512
    snapsems()
    with ExitStack() as _es:
        def _sb(name, shape, dt):
            return _es.enter_context(n.sbuf_tensor(name, shape, dt))

        afm0 = _sb("afm0", [128, 8 * 512], BF16)
        afm1 = _sb("afm1", [128, 8 * 512], BF16)
        wo1 = _sb("wo1", [128, 8 * 1024], BF16)
        bo_row = _sb("bo_row", [1, 1024], F32R)
        x2r0 = _sb("x2r0", [128, 1024], F32)
        x2r1 = _sb("x2r1", [128, 1024], F32)
        x3t0 = _sb("x3t0", [128, 1024], F32)
        x3t1 = _sb("x3t1", [128, 1024], F32)
        ln3 = _sb("ln3", [128, 8], F32)
        ln3s = _sb("ln3s", [128, 1024], F32)
        psO0 = _es.enter_context(n.psum_tensor("psO0", [128, 512], F32))
        psO1 = _es.enter_context(n.psum_tensor("psO1", [128, 512], F32))
        blk = _es.enter_context(n.Block("p2proj"))

        afm = [afm0, afm1]
        x2r = [x2r0, x2r1]
        x3t = [x3t0, x3t1]
        psO = [psO0, psO1]

        o_wl = n.alloc_semaphore("o_wl")
        o_at = n.alloc_semaphore("o_at")
        o_xl = n.alloc_semaphore("o_xl")
        o_pm = n.alloc_semaphore("o_pm")
        o_ev = n.alloc_semaphore("o_ev")
        o_l3 = n.alloc_semaphore("o_l3")
        o_sq = n.alloc_semaphore("o_sq")
        o_fz = n.alloc_semaphore("o_fz")
        o_l3r = n.alloc_semaphore("o_l3r")
        o_fin = n.alloc_semaphore("o_fin")
        o_st = n.alloc_semaphore("o_st")
        o_cst = n.alloc_semaphore("o_cst")

        NTO = SOWN // 128

        @blk.gpsimd
        def _(g):
            g.dma_start(out=wo1[:, :].rearrange("p (kc o) -> p kc o", kc=8),
                        in_=wo_i.rearrange("(kc p) o -> p kc o", p=128)
                        ).then_inc(o_wl, 16)
            g.dma_start(out=bo_row[:, :], in_=bo_i).then_inc(o_wl, 16)
            g.wait_ge(o_st, 16 * NTO)
            g.dma_start(out=x3_bf, in_=x3_tm).then_inc(o_cst, 16)
            g.wait_ge(o_cst, 16)

        @blk.sync
        def _(sy):
            for i in range(NTO + 4):
                k = i - 4
                if 0 <= k < NTO:
                    sy.wait_ge(o_fin, k + 1)
                    sy.dma_start(out=x3_tm[k * 128:(k + 1) * 128, :],
                                 in_=x3t[k % 2][:, :]).then_inc(o_st, 16)
                if i < NTO:
                    if i % 4 == 0:
                        u = i // 4
                        if u >= 2:
                            sy.wait_ge(o_pm, 2 * (u - 1) * 4)
                        for kc in range(8):
                            sy.dma_start_transpose(
                                afm[u % 2][:, kc * 512:(kc + 1) * 512],
                                attn_bf[u * 512:(u + 1) * 512,
                                        kc * 128:(kc + 1) * 128],
                            ).then_inc(o_at, 16)
                    if i >= 2:
                        sy.wait_ge(o_ev, i - 1)
                    sy.dma_start(out=x2r[i % 2][:, :],
                                 in_=x2_tm[(i + 1) * 128:(i + 2) * 128, :]
                                 ).then_inc(o_xl, 16)
            sy.wait_ge(o_st, 16 * NTO)
            sy.wait_ge(o_at, 16 * 8 * NSP)
            sy.wait_ge(o_xl, 16 * NTO)
            sy.wait_ge(o_wl, 32)

        @blk.tensor
        def _(t):
            t.wait_ge(o_wl, 32)
            for i in range(NTO):
                u = i // 4
                k = i % 4
                if k == 0:
                    t.wait_ge(o_at, 16 * 8 * (u + 1))
                if i >= 1:
                    t.wait_ge(o_ev, i)
                for oc in range(2):
                    ins = None
                    for kc in range(8):
                        ins = t.matmul(
                            psO[oc][:, :],
                            afm[u % 2][:, kc * 512 + k * 128:
                                       kc * 512 + k * 128 + 128],
                            wo1[:, kc * 1024 + oc * 512:
                                kc * 1024 + (oc + 1) * 512],
                            start=(kc == 0), stop=False,
                        )
                    ins = t.matmul(
                        psO[oc][:, :], onesr[:, 0:128],
                        bo_row[:, oc * 512:(oc + 1) * 512],
                        start=False, stop=True)
                    ins.then_inc(o_pm, 1)

        @blk.vector
        def _(v):
            fz = _Fence(v, o_fz)
            for i in range(NTO):
                v.wait_ge(o_pm, 2 * i + 2)
                v.wait_ge(o_xl, 16 * (i + 1))
                if i >= 2:
                    v.wait_ge(o_st, 16 * (i - 1))
                v.tensor_add(x3t[i % 2][:, 0:512], psO[0][:, :],
                             x2r[i % 2][:, 0:512])
                v.tensor_add(x3t[i % 2][:, 512:1024], psO[1][:, :],
                             x2r[i % 2][:, 512:1024]).then_inc(o_ev, 1)
                fz(v.tensor_scalar(x3t[i % 2][:, :], x3t[i % 2][:, :], 1.0,
                                   0.0, ALU.mult, ALU.add,
                                   accum_out=ln3[:, 0:1]))
                v.tensor_single_scalar(ln3[:, 0:1], ln3[:, 0:1], INV_C,
                                       ALU.mult)
                v.wait_ge(o_sq, i + 1)
                v.tensor_single_scalar(ln3[:, 1:2], ln3[:, 1:2], INV_C,
                                       ALU.mult)
                fz(v.tensor_tensor(ln3[:, 2:3], ln3[:, 0:1], ln3[:, 0:1],
                                   ALU.mult))
                fz(v.tensor_tensor(ln3[:, 1:2], ln3[:, 1:2], ln3[:, 2:3],
                                   ALU.subtract))
                v.tensor_single_scalar(ln3[:, 1:2], ln3[:, 1:2], LN_EPS,
                                       ALU.add).then_inc(o_l3, 1)
                v.wait_ge(o_l3r, i + 1)
                v.tensor_scalar(x3t[i % 2][:, :], x3t[i % 2][:, :],
                                ln3[:, 0:1], ln3[:, 3:4],
                                ALU.subtract, ALU.mult).then_inc(o_fin, 1)

        @blk.scalar
        def _(s):
            for i in range(NTO):
                s.wait_ge(o_ev, i + 1)
                s.activation(ln3s[:, :], x3t[i % 2][:, :], AF.Square,
                             accum_out=ln3[:, 1:2]).then_inc(o_sq, 1)
                s.wait_ge(o_l3, i + 1)
                s.activation(ln3[:, 3:4], ln3[:, 1:2], AF.Ln)
                s.activation(ln3[:, 3:4], ln3[:, 3:4], AF.Exp,
                             scale=-0.5).then_inc(o_l3r, 1)

    freesems()
    if BUILD_UPTO == 5:
        return nc

    NHALF = 2
    HTOK = SOWN // 2
    NTC = HTOK // 512
    NTT = HTOK // 128
    BIGM = 1e30

    snapsems()
    with ExitStack() as _es:
        def _sb(name, shape, dt):
            return _es.enter_context(n.sbuf_tensor(name, shape, dt))

        x3fm = _sb("x3fm", [128, NTC * 8 * 512], BF16)
        w1s = _sb("w1s", [128, 8 * 2048], BF16)
        w2s0 = _sb("w2s0", [128, 16 * 1024], BF16)
        w2s1 = _sb("w2s1", [128, 16 * 1024], BF16)
        yhalf = _sb("yhalf", [128, NTT * 1024], BF16)
        hm0 = _sb("hm0", [128, 16 * 512], BF16)
        ohT_sb = _sb("ohT_sb", [8, HTOK], BF16)
        maxp_sb = _sb("maxp_sb", [128, NTT], F32)
        b2sb = _sb("b2sb", [8, 1024], BF16)
        br_row = _sb("br_row", [1, 8], F32R)
        b1col = _sb("b1col", [128, 8 * 16], F32)
        wr_sb = _sb("wr_sb", [128, 8 * 8], BF16)
        idsb3 = _sb("idsb3", [128, 128], BF16)
        esel = _sb("esel", [8, 8 * 128], BF16)
        onesb = _sb("onesb", [1, 512], BF16)
        lgt = _sb("lgt", [128, 8], F32)
        rxp = _sb("rxp", [128, 8], F32)
        rcol = _sb("rcol", [128, 4], F32)
        ohsb = _sb("ohsb", [128, 8], BF16)
        mkb0 = _sb("mkb0", [128, 512], BF16)
        x3r0 = _sb("x3r0", [128, 1024], F32)
        x3r1 = _sb("x3r1", [128, 1024], F32)
        x4t0 = _sb("x4t0", [128, 1024], F32)
        x4t1 = _sb("x4t1", [128, 1024], F32)
        ln4 = _sb("ln4", [128, 8], F32)
        psR = _es.enter_context(n.psum_tensor("psR", [128, 512], F32))
        psRT = _es.enter_context(n.psum_tensor("psRT", [8, 128], BF16))
        psH0 = _es.enter_context(n.psum_tensor("psH3", [128, 512], F32))
        psH1 = _es.enter_context(n.psum_tensor("psH4", [128, 512], F32))
        psM = _es.enter_context(n.psum_tensor("psM", [128, 512], F32))
        psY0 = _es.enter_context(n.psum_tensor("psY3", [128, 512], F32))
        psY1 = _es.enter_context(n.psum_tensor("psY4", [128, 512], F32))
        blk = _es.enter_context(n.Block("p3"))

        hm = [hm0, hm0]
        mkb = [mkb0, mkb0]
        x3r = [x3r0, x3r1]
        x4t = [x4t0, x4t1]
        w2s = [w2s0, w2s1]
        psH = [psH0, psH1]
        psY = [psY0, psY1]

        m_cn = n.alloc_semaphore("m_cn")
        m_cs = n.alloc_semaphore("m_cs")
        m_w1 = n.alloc_semaphore("m_w1")
        m_w2 = n.alloc_semaphore("m_w2")
        m_xt = n.alloc_semaphore("m_xt")
        m_rt = n.alloc_semaphore("m_rt")
        m_rx = n.alloc_semaphore("m_rx")
        m_ra = n.alloc_semaphore("m_ra")
        m_ro = n.alloc_semaphore("m_ro")
        m_rt2 = n.alloc_semaphore("m_rt2")
        m_re = n.alloc_semaphore("m_re")
        m_mk = n.alloc_semaphore("m_mk")
        m_mke = n.alloc_semaphore("m_mke")
        m_hm = n.alloc_semaphore("m_hm")
        m_he = n.alloc_semaphore("m_he")
        m_hme = n.alloc_semaphore("m_hme")
        m_ym = n.alloc_semaphore("m_ym")
        m_ye = n.alloc_semaphore("m_ye")
        m_xl = n.alloc_semaphore("m_xl")
        m_fin = n.alloc_semaphore("m_fin")
        m_l4 = n.alloc_semaphore("m_l4")
        m_x4 = n.alloc_semaphore("m_x4")
        m_sq4 = n.alloc_semaphore("m_sq4")
        m_l4r = n.alloc_semaphore("m_l4r")
        m_st = n.alloc_semaphore("m_st")
        m_fz = n.alloc_semaphore("m_fz")

        def rtile(half, r):
            return half * NTT + r

        def ygrp(half, e, tc, g):
            return ((half * 8 + e) * NTC + tc) * 8 + g

        def hgrp(half, e, tc, hc):
            return ((half * 8 + e) * NTC + tc) * 16 + hc

        def mkidx(half, e, tc):
            return (half * 8 + e) * NTC + tc

        @blk.gpsimd
        def _(g):
            g.dma_start(out=b2sb[:, :], in_=b2_i).then_inc(m_cn, 16)
            g.dma_start(out=br_row[:, :], in_=br_i).then_inc(m_cn, 16)
            g.dma_start(out=wr_sb[:, :].rearrange("p (kc o) -> p kc o", kc=8),
                        in_=wr_i.rearrange("(kc p) o -> p kc o", p=128)
                        ).then_inc(m_cn, 16)
            for li in range(16):
                half, e = divmod(li, 8)
                if li >= 1:
                    ph, pe = divmod(li - 1, 8)
                    g.wait_ge(m_hm, hgrp(ph, pe, NTC - 1, 15) + 1)
                g.dma_start(
                    out=w1s[:, :].rearrange("p (kc h) -> p kc h", kc=8),
                    in_=w1_i[e].rearrange("(kc p) h -> p kc h", p=128),
                ).then_inc(m_w1, 16)
                if li >= 2:
                    ph, pe = divmod(li - 2, 8)
                    g.wait_ge(m_ym, ygrp(ph, pe, NTC - 1, 7) + 1)
                g.dma_start(
                    out=w2s[li % 2][:, :].rearrange("p (hc o) -> p hc o",
                                                    hc=16),
                    in_=w2_i[e].rearrange("(hc p) o -> p hc o", p=128),
                ).then_inc(m_w2, 16)
            g.wait_ge(m_w1, 16 * 16)
            g.wait_ge(m_w2, 16 * 16)
            g.wait_ge(m_cn, 48)

        @blk.sync
        def _(sy):
            sy.dma_start(out=idsb3[:, :], in_=identb).then_inc(m_cs, 16)
            sy.dma_start(out=b1col[:, :], in_=b1_col).then_inc(m_cs, 16)
            sy.dma_start(out=esel[:, :], in_=esel_i).then_inc(m_cs, 16)
            for half in range(NHALF):
                if half == 1:
                    sy.wait_ge(m_hm, hgrp(0, 7, NTC - 1, 15) + 1)
                for tc in range(NTC):
                    for kc in range(8):
                        sy.dma_start_transpose(
                            x3fm[:, (tc * 8 + kc) * 512:
                                 (tc * 8 + kc + 1) * 512],
                            x3_bf[half * HTOK + tc * 512:
                                  half * HTOK + (tc + 1) * 512,
                                  kc * 128:(kc + 1) * 128],
                        ).then_inc(m_xt, 16)
                for i in range(NTT):
                    fi = half * NTT + i
                    if fi >= 2:
                        sy.wait_ge(m_sq4, fi - 1)
                    sy.dma_start(
                        out=x3r[fi % 2][:, :],
                        in_=x3_tm[half * HTOK + i * 128:
                                  half * HTOK + (i + 1) * 128, :]
                    ).then_inc(m_xl, 16)
                    if fi >= 2:
                        kf = fi - 2
                        sy.wait_ge(m_fin, kf + 1)
                        sy.dma_start(
                            out=out_c[(kf // NTT) * HTOK + (kf % NTT) * 128:
                                      (kf // NTT) * HTOK + (kf % NTT) * 128
                                      + 128, :],
                            in_=x4t[kf % 2][:, :]).then_inc(m_st, 16)
            for kf in range(NHALF * NTT - 2, NHALF * NTT):
                sy.wait_ge(m_fin, kf + 1)
                sy.dma_start(
                    out=out_c[(kf // NTT) * HTOK + (kf % NTT) * 128:
                              (kf // NTT) * HTOK + (kf % NTT) * 128 + 128, :],
                    in_=x4t[kf % 2][:, :]).then_inc(m_st, 16)
            sy.wait_ge(m_st, 16 * NHALF * NTT)
            sy.wait_ge(m_xt, 16 * NHALF * NTC * 8)
            sy.wait_ge(m_xl, 16 * NHALF * NTT)
            sy.wait_ge(m_cs, 48)

        @blk.tensor
        def _(t):
            t.wait_ge(m_cn, 48)
            t.wait_ge(m_cs, 48)
            for half in range(NHALF):
                t.wait_ge(m_xt, 16 * (half + 1) * NTC * 8)
                for r in range(NTT):
                    ri = rtile(half, r)
                    tc, k = r // 4, r % 4
                    if ri >= 2:
                        t.wait_ge(m_rx, ri - 1)
                    ins = None
                    for kc in range(8):
                        ins = t.matmul(
                            psR[:, (ri % 2) * 8:(ri % 2) * 8 + 8],
                            x3fm[:, (tc * 8 + kc) * 512 + k * 128:
                                 (tc * 8 + kc) * 512 + (k + 1) * 128],
                            wr_sb[:, kc * 8:(kc + 1) * 8],
                            start=(kc == 0), stop=False)
                    ins = t.matmul(
                        psR[:, (ri % 2) * 8:(ri % 2) * 8 + 8],
                        onesr[:, 0:128], br_row[:, :],
                        start=False, stop=True)
                    ins.then_inc(m_rt, 1)
                    if r >= 1:
                        t.wait_ge(m_ro, ri)
                        if ri >= 2:
                            t.wait_ge(m_re, ri - 1)
                        t.transpose(psRT[:, :], ohsb[:, :],
                                    idsb3[:, :]).then_inc(m_rt2, 1)
                t.wait_ge(m_ro, rtile(half, NTT - 1) + 1)
                t.wait_ge(m_re, rtile(half, NTT - 1))
                t.transpose(psRT[:, :], ohsb[:, :],
                            idsb3[:, :]).then_inc(m_rt2, 1)
                for e in range(8):
                    li = half * 8 + e
                    if e == 0:
                        t.wait_ge(m_re, (half + 1) * NTT)
                    t.wait_ge(m_w1, 16 * (li + 1))
                    for tc in range(NTC):
                        mi = mkidx(half, e, tc)
                        if mi >= 2:
                            t.wait_ge(m_mke, mi)
                        t.matmul(psM[:, :],
                                 esel[:, e * 128:(e + 1) * 128],
                                 ohT_sb[:, tc * 512:(tc + 1) * 512],
                                 start=True, stop=True).then_inc(m_mk, 1)
                        for hc in range(16):
                            hg = hgrp(half, e, tc, hc)
                            if hg >= 2:
                                t.wait_ge(m_he, hg - 1)
                            ins = None
                            for kc in range(8):
                                ins = t.matmul(
                                    psH[hc % 2][:, :],
                                    w1s[:, kc * 2048 + hc * 128:
                                        kc * 2048 + (hc + 1) * 128],
                                    x3fm[:, (tc * 8 + kc) * 512:
                                         (tc * 8 + kc + 1) * 512],
                                    start=(kc == 0), stop=(kc == 7),
                                )
                            ins.then_inc(m_hm, 1)
                        t.wait_ge(m_w2, 16 * (li + 1))
                        t.wait_ge(m_hme, hgrp(half, e, tc, 15) + 1)
                        for ts_ in range(4):
                            for oc in range(2):
                                gg = ts_ * 2 + oc
                                yg = ygrp(half, e, tc, gg)
                                if yg >= 2:
                                    t.wait_ge(m_ye, yg - 1)
                                first = (e == 0)
                                ins = None
                                if first:
                                    ins = t.matmul(
                                        psY[gg % 2][:, :],
                                        ohT_sb[:, tc * 512 + ts_ * 128:
                                               tc * 512 + (ts_ + 1) * 128],
                                        b2sb[:, oc * 512:(oc + 1) * 512],
                                        start=True, stop=False)
                                for hc in range(16):
                                    ins = t.matmul(
                                        psY[gg % 2][:, :],
                                        hm[tc % 2][:, hc * 512 + ts_ * 128:
                                                   hc * 512 + (ts_ + 1) * 128],
                                        w2s[li % 2][:, hc * 1024 + oc * 512:
                                                    hc * 1024 + (oc + 1) * 512],
                                        start=(hc == 0 and not first),
                                        stop=(hc == 15),
                                    )
                                ins.then_inc(m_ym, 1)

        @blk.scalar
        def _(s):
            def ln4sA(half, i):
                fi = half * NTT + i
                b4 = 4 * (i % 2)
                s.wait_ge(m_x4, fi + 1)
                s.activation(x3r[fi % 2][:, :],
                             x4t[i % 2][:, :], AF.Square,
                             accum_out=ln4[:, b4 + 1:b4 + 2]
                             ).then_inc(m_sq4, 1)

            def ln4sB(half, i):
                fi = half * NTT + i
                b4 = 4 * (i % 2)
                s.wait_ge(m_l4, fi + 1)
                s.activation(ln4[:, b4 + 3:b4 + 4],
                             ln4[:, b4 + 1:b4 + 2], AF.Ln)
                s.activation(ln4[:, b4 + 3:b4 + 4],
                             ln4[:, b4 + 3:b4 + 4], AF.Exp,
                             scale=-0.5).then_inc(m_l4r, 1)

            def ln4s(half, tc4):
                t0 = 4 * tc4
                ln4sA(half, t0)
                ln4sA(half, t0 + 1)
                ln4sB(half, t0)
                ln4sA(half, t0 + 2)
                ln4sB(half, t0 + 1)
                ln4sA(half, t0 + 3)
                ln4sB(half, t0 + 2)
                ln4sB(half, t0 + 3)

            for half in range(NHALF):
                for r in range(NTT):
                    ri = rtile(half, r)
                    s.wait_ge(m_rx, ri + 1)
                    s.activation(rxp[:, :], lgt[:, :], AF.Exp,
                                 bias=rcol[:, 1:2],
                                 accum_out=rcol[:, 2:3]).then_inc(m_ra, 1)
                for e in range(8):
                    for tc in range(NTC):
                        for hc in range(16):
                            hg = hgrp(half, e, tc, hc)
                            s.wait_ge(m_hm, hg + 1)
                            mi = mkidx(half, e, tc)
                            if mi >= 1 and hc == 0:
                                s.wait_ge(m_ym, mi * 8)
                            s.activation(
                                hm[tc % 2][:, hc * 512:(hc + 1) * 512],
                                psH[hc % 2][:, :], AF.Relu,
                                bias=b1col[:, e * 16 + hc:e * 16 + hc + 1]
                            ).then_inc(m_he, 1)
                        if e == 7 and tc >= 2:
                            ln4s(half, tc - 2)
                ln4s(half, 2)
                ln4s(half, 3)

        @blk.vector
        def _(v):
            fz = _Fence(v, m_fz)

            def ln4vA(half, i):
                fi = half * NTT + i
                b4 = 4 * (i % 2)
                v.wait_ge(m_xl, 16 * (fi + 1))
                if fi >= 2:
                    v.wait_ge(m_st, 16 * (fi - 1))
                ysl = yhalf[:, i * 1024:(i + 1) * 1024]
                v.tensor_single_scalar(ysl, ysl, maxp_sb[:, i:i + 1],
                                       ALU.mult)
                v.tensor_add(x4t[i % 2][:, :], ysl, x3r[fi % 2][:, :])
                v.tensor_scalar(x4t[i % 2][:, :], x4t[i % 2][:, :],
                                1.0, 0.0, ALU.mult, ALU.add,
                                accum_out=ln4[:, b4:b4 + 1]
                                ).then_inc(m_x4, 1)

            def ln4vB(half, i):
                fi = half * NTT + i
                b4 = 4 * (i % 2)
                v.wait_ge(m_x4, fi + 1)
                v.tensor_single_scalar(ln4[:, b4:b4 + 1],
                                       ln4[:, b4:b4 + 1], INV_C, ALU.mult)
                v.wait_ge(m_sq4, fi + 1)
                v.tensor_single_scalar(ln4[:, b4 + 1:b4 + 2],
                                       ln4[:, b4 + 1:b4 + 2], INV_C,
                                       ALU.mult)
                fz(v.tensor_tensor(ln4[:, b4 + 2:b4 + 3], ln4[:, b4:b4 + 1],
                                   ln4[:, b4:b4 + 1], ALU.mult))
                fz(v.tensor_tensor(ln4[:, b4 + 1:b4 + 2],
                                   ln4[:, b4 + 1:b4 + 2],
                                   ln4[:, b4 + 2:b4 + 3], ALU.subtract))
                v.tensor_single_scalar(ln4[:, b4 + 1:b4 + 2],
                                       ln4[:, b4 + 1:b4 + 2], LN_EPS,
                                       ALU.add).then_inc(m_l4, 1)
                v.wait_ge(m_l4r, fi + 1)
                v.tensor_scalar(x4t[i % 2][:, :], x4t[i % 2][:, :],
                                ln4[:, b4:b4 + 1], ln4[:, b4 + 3:b4 + 4],
                                ALU.subtract, ALU.mult).then_inc(m_fin, 1)

            def ln4v(half, tc4):
                t0 = 4 * tc4
                ln4vA(half, t0)
                ln4vA(half, t0 + 1)
                ln4vB(half, t0)
                ln4vA(half, t0 + 2)
                ln4vB(half, t0 + 1)
                ln4vA(half, t0 + 3)
                ln4vB(half, t0 + 2)
                ln4vB(half, t0 + 3)

            v.memset(onesb[:, :], 1.0)
            for half in range(NHALF):
                for r in range(NTT):
                    ri = rtile(half, r)
                    v.wait_ge(m_rt, ri + 1)
                    if half == 1 and r == 0:
                        v.wait_ge(m_ye, ygrp(0, 7, NTC - 1, 7) + 1)
                    if ri >= 1:
                        v.wait_ge(m_ra, ri)
                        v.wait_ge(m_rt2, ri)
                    fz(v.tensor_copy(lgt[:, :],
                                     psR[:, (ri % 2) * 8:(ri % 2) * 8 + 8]))
                    fz(v.reduce_max(rcol[:, 0:1], lgt[:, :], axis=AX.X))
                    v.tensor_single_scalar(rcol[:, 1:2], rcol[:, 0:1], -1.0,
                                           ALU.mult).then_inc(m_rx, 1)
                    v.wait_ge(m_ra, ri + 1)
                    fz(v.reciprocal(rcol[:, 3:4], rcol[:, 2:3]))
                    v.tensor_copy(maxp_sb[:, r:r + 1], rcol[:, 3:4])
                    v.tensor_single_scalar(ohsb[:, :], lgt[:, :],
                                           rcol[:, 0:1],
                                           ALU.is_ge).then_inc(m_ro, 1)
                    v.wait_ge(m_rt2, ri + 1)
                    v.tensor_copy(ohT_sb[:, r * 128:(r + 1) * 128],
                                  psRT[:, :]).then_inc(m_re, 1)
                for e in range(8):
                    for tc in range(NTC):
                        mi = mkidx(half, e, tc)
                        v.wait_ge(m_mk, mi + 1)
                        if mi >= 2:
                            pj = mi - 2
                            phe, ptc = divmod(pj, NTC)
                            v.wait_ge(m_ym, (phe * NTC + ptc) * 8 + 8)
                        v.tensor_single_scalar(mkb[mi % 2][:, :], psM[:, :],
                                               BIGM,
                                               ALU.mult).then_inc(m_mke, 1)
                        for hc in range(16):
                            hg = hgrp(half, e, tc, hc)
                            v.wait_ge(m_he, hg + 1)
                            v.tensor_tensor(
                                hm[tc % 2][:, hc * 512:(hc + 1) * 512],
                                hm[tc % 2][:, hc * 512:(hc + 1) * 512],
                                mkb[mi % 2][:, :],
                                ALU.min).then_inc(m_hme, 1)
                        for gg in range(8):
                            ts_, oc = gg // 2, gg % 2
                            yg = ygrp(half, e, tc, gg)
                            v.wait_ge(m_ym, yg + 1)
                            ysl = yhalf[:, (tc * 4 + ts_) * 1024 + oc * 512:
                                        (tc * 4 + ts_) * 1024 +
                                        (oc + 1) * 512]
                            if e == 0:
                                v.tensor_copy(ysl, psY[gg % 2][:, :]
                                              ).then_inc(m_ye, 1)
                            else:
                                v.tensor_add(ysl, ysl, psY[gg % 2][:, :]
                                             ).then_inc(m_ye, 1)
                        if e == 7 and tc >= 2:
                            ln4v(half, tc - 2)
                ln4v(half, 2)
                ln4v(half, 3)

    return nc



def _host_consts(inputs, core):
    b, h = core // 2, core % 2
    rpb = np.asarray(inputs["L1_attn_rel_pos_bias"], np.float32)
    w = np.arange(128)[:, None]
    q = np.arange(128)[None, :]
    biasT = np.empty((H1, 3, 128, 128), np.float32)
    for r in range(3):
        delta = 128 * r + w - 128 - q
        idx = np.clip(delta + 128, 0, 256)
        valid = (delta >= -128) & (delta <= 128)
        for hh in range(H1):
            biasT[hh, r] = np.where(valid, rpb[hh][idx], NEG)
    biasTL = biasT[:, 0].copy()
    biasTR = biasT[:, 2].copy()
    if h == 0:
        biasTL[:] = NEG
    else:
        biasTR[:] = NEG
    return {
        "biasT_i": np.ascontiguousarray(biasT),
        "biasTL_i": np.ascontiguousarray(biasTL),
        "biasTR_i": np.ascontiguousarray(biasTR),
        "identb": np.eye(128, dtype=np.float32).astype(BF),
    }


def _core_inputs(inputs, core):
    b, h = core // 2, core % 2
    x = np.ascontiguousarray(inputs["L0_attn_x"][b])
    lo = h * SOWN - MARG
    idx = (np.arange(lo, lo + SEXT)) % S
    x_ext = np.ascontiguousarray(x[idx])
    d = {
        "x_all": x,
        "x_ext": x_ext,
        "w_qkv": np.ascontiguousarray(inputs["L0_attn_qkv_weight"]),
        "w_proj": np.ascontiguousarray(inputs["L0_attn_proj_weight"]),
        "b_proj": np.ascontiguousarray(inputs["L0_attn_proj_bias"])
        .reshape(1, C),
        "w_ffn": np.ascontiguousarray(inputs["L0_ffn_weight"]),
        "b_ffn": np.ascontiguousarray(inputs["L0_ffn_bias"]).reshape(1, C),
        "ones128": np.ones((1, 512), np.float32),
    }
    if BUILD_UPTO >= 3:
        d.update({
            "wq_i": np.ascontiguousarray(inputs["L1_attn_Wq"]),
            "wk_i": np.ascontiguousarray(inputs["L1_attn_Wk"]),
            "wv_i": np.ascontiguousarray(inputs["L1_attn_Wv"]),
            "bv_i": np.ascontiguousarray(inputs["L1_attn_bv"]).reshape(1, C),
            "bqc_i": np.ascontiguousarray(
                np.asarray(inputs["L1_attn_bq"]).reshape(8, 128).T),
            "bkc_i": np.ascontiguousarray(
                np.asarray(inputs["L1_attn_bk"]).reshape(8, 128).T),
        })
    if BUILD_UPTO >= 4:
        d.update(_host_consts(inputs, core))
    if BUILD_UPTO >= 5:
        d.update({
            "wo_i": np.ascontiguousarray(inputs["L1_attn_Wo"]),
            "bo_i": np.ascontiguousarray(inputs["L1_attn_bo"]).reshape(1, C),
        })
    if BUILD_UPTO >= 6:
        d.update({
            "wr_i": np.ascontiguousarray(inputs["L1_ffn_Wr"]),
            "br_i": np.ascontiguousarray(inputs["L1_ffn_br"]).reshape(1, 8),
            "w1_i": np.ascontiguousarray(inputs["L1_ffn_W1"]),
            "b1_col": np.ascontiguousarray(
                np.asarray(inputs["L1_ffn_b1"]).reshape(8, 16, 128)
                .transpose(2, 0, 1).reshape(128, 128)),
            "w2_i": np.ascontiguousarray(inputs["L1_ffn_W2"]),
            "b2_i": np.ascontiguousarray(inputs["L1_ffn_b2"]),
            "esel_i": np.ascontiguousarray(
                np.eye(8, dtype=np.float32)[:, :, None]
                .repeat(128, axis=2).reshape(8, 8 * 128).astype(BF)),
        })
    return d


def kernel(**inputs):
    nc = build_program()
    in_maps = [_core_inputs(inputs, c) for c in range(8)]
    res = run_bass_kernel_spmd(nc, in_maps, core_ids=list(range(8)))
    if BUILD_UPTO < 6:
        return res
    out = np.empty((B, S, C), np.float32)
    for c in range(8):
        b, h = c // 2, c % 2
        out[b, h * SOWN:(h + 1) * SOWN, :] = res.results[c]["out_c"]
    return out
